# version 06
# baseline (speedup 1.0000x reference)
import math
import numpy as np
import contextlib
import concourse.bass as bass
import concourse.mybir as mybir
from concourse.bass_utils import run_bass_kernel_spmd
from concourse.alu_op_type import AluOpType

F32 = mybir.dt.float32
AF = mybir.ActivationFunctionType
AX = mybir.AxisListType
OP = AluOpType

B, T, D, HFF = 2, 4096, 4096, 11008
NG = 4
TOWN = 1024
NHT = HFF // 128
EPS = 1e-5

CODE64 = np.array([-1.0, -0.6961928009986877, -0.5250730514526367, -0.39491748809814453,
                   -0.28444138169288635, -0.18477343022823334, -0.09105003625154495, 0.0,
                   0.07958029955625534, 0.16093020141124725, 0.24611230194568634,
                   0.33791524171829224, 0.44070982933044434, 0.5626170039176941,
                   0.7229568362236023, 1.0])
B64 = (CODE64[:-1] + CODE64[1:]) * 0.5
BHI = B64.astype(np.float32)
BLO = (B64 - BHI.astype(np.float64)).astype(np.float32)
DLT = np.diff(CODE64).astype(np.float32)
C0 = float(np.float32(CODE64[0]))

ENGS = ("sync", "tensor", "vector", "scalar", "gpsimd")


class Plan:
    def __init__(self):
        self.items = {e: [] for e in ENGS}
        self.cnt = {"ld": 0, "st": 0, "t": 0, "v": 0, "a": 0, "lg": 0, "sg": 0}
        self.esem = {"sync": "ld", "gpsimd": "st", "tensor": "t", "vector": "v", "scalar": "a"}

    def emit(self, eng, fn, waits=(), inc=False):
        q = 16 if eng in ("sync", "gpsimd") else 1
        sem = self.esem[eng]
        if eng in ("vector", "scalar"):
            inc = True
        self.items[eng].append((list(waits), fn, sem if inc else None, q))
        if inc:
            self.cnt[sem] += q

    def c(self, s):
        return self.cnt[s]


def build_program():
    nc = bass.Bass("TRN2", target_bir_lowering=False)
    dt = nc.dram_tensor
    x_own = dt("x_own", [TOWN, D], F32, kind="ExternalInput").ap()
    x_cov = dt("x_cov", [T, D], F32, kind="ExternalInput").ap()
    Wq = dt("Wq", [NG, 32, 128, 1024], F32, kind="ExternalInput").ap()
    Wkv = dt("Wkv", [NG, 32, 128, 512], F32, kind="ExternalInput").ap()
    Wo = dt("Wo", [D, D], F32, kind="ExternalInput").ap()
    w1 = dt("w1", [HFF, D], F32, kind="ExternalInput").ap()
    w2 = dt("w2", [HFF, D], F32, kind="ExternalInput").ap()
    w3 = dt("w3", [D, HFF], F32, kind="ExternalInput").ap()
    rope_own = dt("rope_own", [4, TOWN, 64], F32, kind="ExternalInput").ap()
    rope_cov = dt("rope_cov", [4, T, 64], F32, kind="ExternalInput").ap()
    m01 = dt("m01", [12, 4, 128, 512], F32, kind="ExternalInput").ap()
    ident_in = dt("ident", [128, 128], F32, kind="ExternalInput").ap()
    ones_in = dt("ones", [128, 128], F32, kind="ExternalInput").ap()
    out = dt("out", [TOWN, D], F32, kind="ExternalOutput").ap()

    sc_xnT_cov = dt("sc_xnT_cov", [8, 128, 16384], F32).ap()
    sc_xnT_own = dt("sc_xnT_own", [2, 128, 16384], F32).ap()
    sc_dq3T = dt("sc_dq3T", [NHT, 32, 128, 128], F32).ap()
    sc_hT = dt("sc_hT", [NHT, 128, 1024], F32).ap()
    sc_attn = dt("sc_attn", [NG, 8, 128, 4096], F32).ap()

    P = Plan()
    with contextlib.ExitStack() as es:
        sbuf = lambda name, cols: es.enter_context(nc.sbuf_tensor(name, [128, cols], F32))
        AR1 = sbuf("AR1", 24576)
        AR2 = sbuf("AR2", 16384)
        AR3 = sbuf("AR3", 5120)
        SM = sbuf("SM", 192)
        RPO = sbuf("RPO", 256)
        RPC = sbuf("RPC", 256)
        IDT = sbuf("IDT", 128)
        ONE = sbuf("ONE", 128)
        SUMS = sbuf("SUMS", 2048)
        pb = [es.enter_context(nc.psum_tensor("pb%d" % i, [128, 512], F32)) for i in range(8)]
        sem = {k: es.enter_context(nc.semaphore("sem_" + k))
               for k in ("ld", "st", "t", "v", "a", "lg", "sg")}
        block = es.enter_context(nc.Block())

        nc.const_aps.register(F32, EPS, SM[:, 8:9])

        def LD(dst, src, waits=(), qe="sync"):
            P.emit(qe, lambda e, d=dst, s=src: e.dma_start(out=d, in_=s), waits, True)

        def ST(dst, src, waits=()):
            P.emit("gpsimd", lambda e, d=dst, s=src: e.dma_start(out=d, in_=s), waits, True)

        def LMARK(qe="sync"):
            sm = "ld" if qe == "sync" else "st"
            P.emit(qe, lambda e: e.nop(), [(sm, P.c(sm))], False)
            return (sm, P.c(sm))

        def SMARK():
            P.emit("gpsimd", lambda e: e.nop(), [("st", P.c("st"))], False)
            return ("st", P.c("st"))

        TE = lambda fn, waits=(), inc=False: P.emit("tensor", fn, waits, inc)
        VE = lambda fn, waits=(), inc=False: P.emit("vector", fn, waits, inc)
        AE = lambda fn, waits=(), inc=False: P.emit("scalar", fn, waits, inc)

        VE(lambda e: e.memset(SM[:, 8:9], EPS))
        LD(IDT[:], ident_in)
        LD(ONE[:], ones_in)
        w_consts = [LMARK(), ("v", P.c("v"))]

        bx = AR1[:, 0:4096]
        bsq = AR1[:, 4096:8192]
        bxn = AR1[:, 8192:12288]
        basm = AR2[:, 0:16384]
        s_sum, s_sq = SM[:, 0:1], SM[:, 1:2]
        s_m, s_ex2 = SM[:, 2:3], SM[:, 3:4]
        s_mm, s_var = SM[:, 4:5], SM[:, 5:6]
        s_sd, s_r = SM[:, 6:7], SM[:, 7:8]
        s_mr, s_b = SM[:, 10:11], SM[:, 11:12]

        blocks = [("cov", kb) for kb in range(8)] + [("own", h) for h in range(2)]
        xnT_ready = {}
        for bid, (kind, idx) in enumerate(blocks):
            src = x_cov if kind == "cov" else x_own
            for ts_ in range(4):
                row0 = idx * 512 + ts_ * 128
                qe = "sync" if ts_ % 2 == 0 else "gpsimd"
                LD(bx, src[row0:row0 + 128, :],
                   waits=[("v", P.c("v")), ("a", P.c("a"))] + (w_consts if bid == 0 and ts_ == 0 else []),
                   qe=qe)
                wld = LMARK(qe)
                VE(lambda e: e.tensor_reduce(out=s_sum, in_=bx, axis=AX.X, op=OP.add),
                   waits=[wld])
                AE(lambda e: e.activation(bsq, bx, AF.Square),
                   waits=[wld, ("v", P.c("v"))], inc=True)
                VE(lambda e: e.tensor_reduce(out=s_sq, in_=bsq, axis=AX.X, op=OP.add),
                   waits=[("a", P.c("a"))])
                VE(lambda e: e.tensor_scalar_mul(s_m, s_sum, 1.0 / D))
                VE(lambda e: e.tensor_scalar_mul(s_ex2, s_sq, 1.0 / D))
                VE(lambda e: e.tensor_tensor(out=s_mm, in0=s_m, in1=s_m, op=OP.mult))
                VE(lambda e: e.tensor_tensor(out=s_var, in0=s_ex2, in1=s_mm, op=OP.subtract), inc=True)
                AE(lambda e: e.activation(s_sd, s_var, AF.Sqrt, bias=EPS),
                   waits=[("v", P.c("v"))], inc=True)
                VE(lambda e: e.reciprocal(s_r, s_sd), waits=[("a", P.c("a"))])
                VE(lambda e: e.tensor_tensor(out=s_mr, in0=s_m, in1=s_r, op=OP.mult))
                VE(lambda e: e.tensor_scalar_mul(s_b, s_mr, -1.0), inc=True)
                AE(lambda e: e.activation(bxn, bx, AF.Identity, bias=s_b, scale=s_r),
                   waits=[("v", P.c("v"))], inc=True)
                for grp in range(8):
                    for j in range(4):
                        ds = grp * 4 + j
                        TE(lambda e, j=j, ds=ds: e.transpose(
                            pb[6][:, j * 128:(j + 1) * 128], bxn[:, ds * 128:(ds + 1) * 128], IDT[:]),
                           waits=([("a", P.c("a"))] if j == 0 else []), inc=(j == 3))
                    wt = P.c("t")
                    for j in range(4):
                        ds = grp * 4 + j
                        cc = ds * 512 + ts_ * 128
                        AE(lambda e, j=j, cc=cc: e.copy(
                            basm[:, cc:cc + 128], pb[6][:, j * 128:(j + 1) * 128]),
                           waits=[("t", wt)], inc=(j == 3))
            dstb = (sc_xnT_cov if kind == "cov" else sc_xnT_own)[idx]
            ST(dstb, basm, waits=[("a", P.c("a"))])
            xnT_ready[(kind, idx)] = SMARK()
            if bid + 1 < len(blocks):
                AE(lambda e: e.nop(), waits=[xnT_ready[(kind, idx)]], inc=True)

        bwaits = [LMARK(), SMARK()] + [(k, P.c(k)) for k in ("t", "v", "a")]
        P.emit("sync", lambda e: e.nop(), [w for w in bwaits if w[0] != "lg"], False)
        P.emit("gpsimd", lambda e: e.nop(), [w for w in bwaits if w[0] != "sg"], False)
        P.emit("tensor", lambda e: e.nop(), [w for w in bwaits if w[0] != "t"], True)
        P.emit("vector", lambda e: e.nop(), [w for w in bwaits if w[0] != "v"], True)
        P.emit("scalar", lambda e: e.nop(), [w for w in bwaits if w[0] != "a"], True)

        bprobs = AR1[:, 8192:10240]
        m1b = bprobs[:, 0:512]
        m2b = bprobs[:, 512:1024]
        o1b = bprobs[:, 1024:1536]
        o2b = bprobs[:, 1536:2048]
        braw = AR1[:, 10240:12288]
        bvv = AR1[:, 12288:14336]
        btmp = AR1[:, 14336:16384]
        bacc = [AR1[:, 16384:18432], AR1[:, 18432:20480]]
        bdqa = AR1[:, 20480:24576]
        s_am = SM[:, 64:128]
        s_rc = SM[:, 128:192]
        qTb = AR1[:, 0:8192]
        kTb = AR2[:, 0:8192]
        vTb = AR2[:, 8192:16384]
        bwo = AR2[:, 0:8192]
        batn = AR2[:, 8192:12288]
        xa = [AR3[:, 0:512], AR3[:, 512:1024]]
        xb = [AR3[:, 1024:1536], AR3[:, 1536:2048]]
        wpc = [AR3[:, 2048:3072], AR3[:, 3072:4096]]
        bqro = AR3[:, 4096:4608]
        bmsk = AR3[:, 4608:5120]
        rop = [RPO[:, i * 64:(i + 1) * 64] for i in range(4)]
        rcv = [RPC[:, i * 64:(i + 1) * 64] for i in range(4)]
        attn_mark = [None]

        spill = {}
        dq_state = {"wt_prev": 0}

        dqst = {"cons": [0, 0], "slot": 0, "stg": 0}

        def dequant_job(tag, idx, dst_asm=None, asm_guard=(), stg=None):
            if tag == "w1":
                src = w1
                chunks = [(k * 1024, 1024) for k in range(4)]
            elif tag == "w2":
                src = w2
                chunks = [(k * 1024, 1024) for k in range(4)]
            else:
                src = w3
                chunks = [(k * 1024, 1024) for k in range(10)] + [(10240, 768)]
            first_chunk = True
            for (c0, cw) in chunks:
                nb = cw // 64
                slot = dqst["slot"]
                dqst["slot"] = 1 - slot
                brw = braw[slot]
                qe = "sync" if slot == 0 else "gpsimd"
                LD(brw[:, 0:cw], src[idx * 128:(idx + 1) * 128, c0:c0 + cw],
                   waits=[("v", dqst["cons"][slot])], qe=qe)
                wld = LMARK(qe)
                rawv = brw[:, 0:cw].rearrange("p (b j) -> p b j", j=64)
                VE(lambda e, rawv=rawv, nb=nb: e.tensor_reduce(
                    out=s_am[:, 0:nb], in_=rawv, axis=AX.X, op=OP.max, apply_absolute_value=True),
                   waits=[wld])
                VE(lambda e, nb=nb: e.reciprocal(s_rc[:, 0:nb], s_am[:, 0:nb]))
                rbc = s_rc[:, 0:nb].unsqueeze(2).broadcast_to((128, nb, 64))
                VE(lambda e, rawv=rawv, rbc=rbc, cw=cw: e.tensor_tensor(
                    out=bvv[:, 0:cw].rearrange("p (b j) -> p b j", j=64), in0=rawv, in1=rbc,
                    op=OP.mult), inc=True)
                dqst["cons"][slot] = P.c("v")
                VE(lambda e, cw=cw: e.tensor_scalar(out=btmp[:, 0:cw], in0=bvv[:, 0:cw],
                                                    scalar1=float(BHI[0]), scalar2=float(BLO[0]),
                                                    op0=OP.subtract, op1=OP.is_gt))
                VE(lambda e, cw=cw: e.tensor_scalar_mul(bacc[0][:, 0:cw], btmp[:, 0:cw],
                                                        float(DLT[0])),
                   waits=[("t", dq_state["wt_prev"])])
                a = 0
                for i in range(1, 15):
                    VE(lambda e, cw=cw, i=i: e.tensor_scalar(
                        out=btmp[:, 0:cw], in0=bvv[:, 0:cw], scalar1=float(BHI[i]),
                        scalar2=float(BLO[i]), op0=OP.subtract, op1=OP.is_gt))
                    VE(lambda e, cw=cw, i=i, a=a: e.scalar_tensor_tensor(
                        out=bacc[1 - a][:, 0:cw], in0=btmp[:, 0:cw], scalar=float(DLT[i]),
                        in1=bacc[a][:, 0:cw], op0=OP.mult, op1=OP.add))
                    a = 1 - a
                abc = s_am[:, 0:nb].unsqueeze(2).broadcast_to((128, nb, 64))
                VE(lambda e, cw=cw, a=a, abc=abc: e.scalar_tensor_tensor(
                    out=bacc[1 - a][:, 0:cw].rearrange("p (b j) -> p b j", j=64),
                    in0=bacc[a][:, 0:cw].rearrange("p (b j) -> p b j", j=64),
                    scalar=C0, in1=abc, op0=OP.add, op1=OP.mult), inc=True)
                wdq = P.c("v")
                dqbuf = bacc[1 - a]
                npc = cw // 128
                for p0 in range(0, npc, 4):
                    n4 = min(4, npc - p0)
                    for j in range(n4):
                        TE(lambda e, j=j, dqbuf=dqbuf, p0=p0: e.transpose(
                            pb[7][:, j * 128:(j + 1) * 128],
                            dqbuf[:, (p0 + j) * 128:(p0 + j + 1) * 128], IDT[:]),
                           waits=([("v", wdq), ("a", P.c("a"))] if j == 0 else []),
                           inc=(j == n4 - 1))
                    wt = P.c("t")
                    ev_waits = [("t", wt)]
                    if tag != "w3":
                        if first_chunk and p0 == 0:
                            ev_waits.extend(asm_guard)
                        for j in range(n4):
                            gp = c0 // 128 + p0 + j
                            AE(lambda e, j=j, gp=gp, dst_asm=dst_asm: e.copy(
                                dst_asm[:, gp * 128:(gp + 1) * 128], pb[7][:, j * 128:(j + 1) * 128]),
                               waits=(ev_waits if j == 0 else []), inc=(j == n4 - 1))
                    else:
                        ev_waits.append(SMARK())
                        for j in range(n4):
                            AE(lambda e, j=j, stg=stg: e.copy(
                                stg[:, j * 128:(j + 1) * 128], pb[7][:, j * 128:(j + 1) * 128]),
                               waits=(ev_waits if j == 0 else []), inc=(j == n4 - 1))
                        wa = P.c("a")
                        for j in range(n4):
                            ht = c0 // 128 + p0 + j
                            ST(sc_dq3T[ht, idx], stg[:, j * 128:(j + 1) * 128], waits=[("a", wa)])
                dq_state["wt_prev"] = P.c("t")
                first_chunk = False
            if tag == "w3":
                spill[(tag, idx)] = SMARK()



        def rope_apply(mk, nh, tabs, dst, dstoff, waits0):
            ch, cl, sh, sl = tabs
            bc = lambda t: t.unsqueeze(1).broadcast_to((128, nh, 64))
            x1 = mk(0, 64)
            x2 = mk(64, 128)
            dview = dst[:, dstoff:dstoff + nh * 128].rearrange("p (h j) -> p h j", j=128)
            dst1 = dview[:, :, 0:64]
            dst2 = dview[:, :, 64:128]
            mm1 = m1b[:, 0:nh * 64].rearrange("p (h j) -> p h j", j=64)
            mm2 = m2b[:, 0:nh * 64].rearrange("p (h j) -> p h j", j=64)
            oo1 = o1b[:, 0:nh * 64].rearrange("p (h j) -> p h j", j=64)
            oo2 = o2b[:, 0:nh * 64].rearrange("p (h j) -> p h j", j=64)
            VE(lambda e: e.tensor_tensor(out=mm1, in0=x1, in1=bc(ch), op=OP.mult), waits=waits0)
            VE(lambda e: e.tensor_tensor(out=mm2, in0=x2, in1=bc(sh), op=OP.mult))
            VE(lambda e: e.tensor_tensor(out=oo1, in0=mm1, in1=mm2, op=OP.subtract))
            VE(lambda e: e.tensor_tensor(out=mm1, in0=x1, in1=bc(cl), op=OP.mult))
            VE(lambda e: e.tensor_tensor(out=mm2, in0=x2, in1=bc(sl), op=OP.mult))
            VE(lambda e: e.tensor_tensor(out=oo2, in0=mm1, in1=mm2, op=OP.subtract))
            VE(lambda e: e.tensor_tensor(out=dst1, in0=oo1, in1=oo2, op=OP.add))
            VE(lambda e: e.tensor_tensor(out=mm1, in0=x1, in1=bc(sh), op=OP.mult))
            VE(lambda e: e.tensor_tensor(out=mm2, in0=x2, in1=bc(ch), op=OP.mult))
            VE(lambda e: e.tensor_tensor(out=oo1, in0=mm1, in1=mm2, op=OP.add))
            VE(lambda e: e.tensor_tensor(out=mm1, in0=x1, in1=bc(sl), op=OP.mult))
            VE(lambda e: e.tensor_tensor(out=mm2, in0=x2, in1=bc(cl), op=OP.mult))
            VE(lambda e: e.tensor_tensor(out=oo2, in0=mm1, in1=mm2, op=OP.add))
            VE(lambda e: e.tensor_tensor(out=dst2, in0=oo1, in1=oo2, op=OP.add), inc=True)

        for g in range(NG):
            for pr in range(4):
                for ds in range(32):
                    rot = ds % 2
                    qe = "sync" if rot == 0 else "gpsimd"
                    LD(wpc[rot][:, 0:1024], Wq[g, ds], waits=[("t", P.c("t"))], qe=qe)
                    for ti in range(2):
                        tile = pr * 2 + ti
                        half, tsub = tile // 4, tile % 4
                        cc = ds * 512 + tsub * 128
                        LD((xa if ti == 0 else xb)[rot][:, 0:128],
                           sc_xnT_own[half][:, cc:cc + 128],
                           waits=[xnT_ready[("own", half)], ("t", P.c("t"))], qe=qe)
                    wld = LMARK(qe)
                    for ti in range(2):
                        xsrc = (xa if ti == 0 else xb)[rot][:, 0:128]
                        for nbk in range(2):
                            TE(lambda e, ti=ti, nbk=nbk, xsrc=xsrc, rot=rot, ds=ds: e.matmul(
                                pb[2 * ti + nbk][:], xsrc,
                                wpc[rot][:, nbk * 512:(nbk + 1) * 512],
                                start=(ds == 0), stop=(ds == 31)),
                               waits=([wld] +
                                      ([("v", P.c("v")), ("a", P.c("a"))] if ds == 0 else [])),
                               inc=(ti == 1 and nbk == 1))
                wmm = P.c("t")
                for ti in range(2):
                    tile = pr * 2 + ti
                    qe = "sync" if ti % 2 == 0 else "gpsimd"
                    for i in range(4):
                        LD(rop[i], rope_own[i, tile * 128:(tile + 1) * 128, :],
                           waits=[("v", P.c("v"))], qe=qe)
                    wld2 = LMARK(qe)
                    for bk in range(2):
                        bank = pb[2 * ti + bk]
                        mk = lambda lo, hi, bank=bank: bank[:, :].rearrange(
                            "p (h j) -> p h j", j=128)[:, :, lo:hi]
                        rope_apply(mk, 4, rop, bqro, 0,
                                   [wld2, ("t", wmm), ("a", P.c("a"))])
                        wv = P.c("v")
                        for j in range(4):
                            TE(lambda e, j=j: e.transpose(
                                pb[4][:, j * 128:(j + 1) * 128], bqro[:, j * 128:(j + 1) * 128],
                                IDT[:]),
                               waits=([("v", wv), ("a", P.c("a"))] if j == 0 else []),
                               inc=(j == 3))
                        wt = P.c("t")
                        for j in range(4):
                            h = bk * 4 + j
                            c1 = h * 1024 + tile * 128
                            AE(lambda e, j=j, c1=c1: e.copy(
                                qTb[:, c1:c1 + 128], pb[4][:, j * 128:(j + 1) * 128]),
                               waits=[("t", wt)], inc=(j == 3))
            for kb in range(8):
                for ds in range(32):
                    rot = ds % 2
                    qe = "sync" if rot == 0 else "gpsimd"
                    LD(wpc[rot][:, 0:512], Wkv[g, ds], waits=[("t", P.c("t"))], qe=qe)
                    LD(xa[rot][:, 0:512], sc_xnT_cov[kb][:, ds * 512:ds * 512 + 512],
                       waits=[xnT_ready[("cov", kb)], ("t", P.c("t"))], qe=qe)
                    wld = LMARK(qe)
                    for tks in range(4):
                        TE(lambda e, tks=tks, rot=rot, ds=ds: e.matmul(
                            pb[tks][:], xa[rot][:, tks * 128:(tks + 1) * 128],
                            wpc[rot][:, 0:512], start=(ds == 0), stop=(ds == 31)),
                           waits=([wld] +
                                  ([("v", P.c("v")), ("a", P.c("a"))] if ds == 0 else [])),
                           inc=(tks == 3))
                wmm = P.c("t")
                for tks in range(4):
                    qe = "sync" if tks % 2 == 0 else "gpsimd"
                    for i in range(4):
                        LD(rcv[i], rope_cov[i, kb * 512 + tks * 128: kb * 512 + (tks + 1) * 128, :],
                           waits=[("v", P.c("v"))], qe=qe)
                    wld2 = LMARK(qe)
                    bank = pb[tks]
                    mk = lambda lo, hi, bank=bank: bank[:, 0:256].rearrange(
                        "p (h j) -> p h j", j=128)[:, :, lo:hi]
                    rope_apply(mk, 2, rcv, bqro, 0, [wld2, ("t", wmm), ("a", P.c("a"))])
                    wv = P.c("v")
                    for j in range(2):
                        TE(lambda e, j=j: e.transpose(
                            pb[4][:, j * 128:(j + 1) * 128], bqro[:, j * 128:(j + 1) * 128], IDT[:]),
                           waits=([("v", wv), ("a", P.c("a"))] if j == 0 else []), inc=(j == 1))
                    wt = P.c("t")
                    for j in range(2):
                        c1 = j * 4096 + kb * 512 + tks * 128
                        AE(lambda e, j=j, c1=c1: e.copy(
                            kTb[:, c1:c1 + 128], pb[4][:, j * 128:(j + 1) * 128]),
                           waits=[("t", wt)] +
                                 ([attn_mark[0]] if (kb == 0 and tks == 0 and j == 0
                                                     and attn_mark[0]) else []),
                           inc=(j == 1))
                    for j in range(2):
                        c1 = ((kb * 4 + tks) * 2 + j) * 128
                        AE(lambda e, j=j, c1=c1, tks=tks: e.copy(
                            vTb[:, c1:c1 + 128], pb[tks][:, 256 + j * 128:256 + (j + 1) * 128]),
                           inc=(j == 1))
            bank_exp = [0, 0]
            last_mult_v = 0
            last_avsums_t = 0
            for h in range(8):
                kvh = h // 4
                hfirst = [("a", P.c("a")), ("v", P.c("v"))]
                for qc in range(2):
                    nkb = 4 if qc == 0 else 8
                    nsteps = nkb * 4
                    step = 0
                    for kb in range(nkb):
                        blkid = kb if qc == 0 else 4 + kb
                        for tks in range(4):
                            first = (step == 0)
                            last = (step == nsteps - 1)
                            sbank = pb[step % 2]
                            kcol = kvh * 4096 + kb * 512 + tks * 128
                            qcol = h * 1024 + qc * 512
                            TE(lambda e, sbank=sbank, kcol=kcol, qcol=qcol: e.matmul(
                                sbank[:], kTb[:, kcol:kcol + 128], qTb[:, qcol:qcol + 512],
                                start=True, stop=True),
                               waits=[("a", bank_exp[step % 2])] +
                                     (hfirst if (first and qc == 0) else []),
                               inc=True)
                            wsc = P.c("t")
                            AE(lambda e, sbank=sbank: e.activation(
                                bprobs[:, 0:512], sbank[:], AF.Exp),
                               waits=[("t", wsc), ("v", last_mult_v)], inc=True)
                            wex = P.c("a")
                            bank_exp[step % 2] = wex
                            qe = "sync" if step % 2 == 0 else "gpsimd"
                            LD(bmsk, m01[blkid, tks], waits=[("v", last_mult_v)], qe=qe)
                            wmk = LMARK(qe)
                            VE(lambda e: e.tensor_tensor(
                                out=bqro[:], in0=bprobs[:, 0:512], in1=bmsk[:], op=OP.mult),
                               waits=[("a", wex), wmk, ("t", last_avsums_t)], inc=True)
                            wpm = P.c("v")
                            last_mult_v = wpm
                            vcol = ((kb * 4 + tks) * 2 + kvh) * 128
                            TE(lambda e, vcol=vcol, qc=qc, first=first, last=last: e.matmul(
                                pb[2 + qc][:], vTb[:, vcol:vcol + 128], bqro[:],
                                start=first, stop=last),
                               waits=[("v", wpm)], inc=True)
                            TE(lambda e, qc=qc, first=first, last=last: e.matmul(
                                pb[4 + qc][0:1, :], ONE[:, 0:1], bqro[:],
                                start=first, stop=last),
                               inc=True)
                            last_avsums_t = P.c("t")
                            step += 1
                wavd = P.c("t")
                AE(lambda e: e.copy(SUMS[0:1, 0:512], pb[4][0:1, :]), waits=[("t", wavd)])
                AE(lambda e: e.copy(SUMS[0:1, 512:1024], pb[5][0:1, :]), inc=True)
                VE(lambda e: e.reciprocal(SUMS[0:1, 1024:2048], SUMS[0:1, 0:1024]),
                   waits=[("a", P.c("a"))], inc=True)
                wrs = P.c("v")
                for qc in range(2):
                    TE(lambda e, qc=qc: e.matmul(
                        pb[qc][:], ONE[0:1, 0:128], SUMS[0:1, 1024 + qc * 512:1024 + (qc + 1) * 512],
                        start=True, stop=True),
                       waits=([("v", wrs), ("a", P.c("a"))] if qc == 0 else []), inc=(qc == 1))
                wbc = P.c("t")
                for qc in range(2):
                    AE(lambda e, qc=qc: e.copy(bprobs[:, qc * 512:(qc + 1) * 512], pb[qc][:]),
                       waits=([("t", wbc), ("v", P.c("v"))] if qc == 0 else []), inc=(qc == 1))
                wev = P.c("a")
                for qc in range(2):
                    c1 = h * 1024 + qc * 512
                    VE(lambda e, qc=qc, c1=c1: e.tensor_tensor(
                        out=qTb[:, c1:c1 + 512], in0=pb[2 + qc][:],
                        in1=bprobs[:, qc * 512:(qc + 1) * 512], op=OP.mult),
                       waits=[("a", wev)], inc=(qc == 1))
            for tt_ in range(8):
                for rnd, (nb_, coff) in enumerate([(6, 0), (2, 3072)]):
                    for s in range(8):
                        rot = s % 2
                        qe = "sync" if rot == 0 else "gpsimd"
                        LD(bwo[:, rot * 4096: rot * 4096 + nb_ * 512],
                           Wo[g * 1024 + s * 128: g * 1024 + s * 128 + 128, coff:coff + nb_ * 512],
                           waits=[("t", P.c("t"))] +
                                 ([("v", P.c("v"))] if tt_ == 0 and rnd == 0 and s == 0 else []),
                           qe=qe)
                        wld = LMARK(qe)
                        for c in range(nb_):
                            lcol = (g * 0 + s) * 1024 + tt_ * 128
                            TE(lambda e, c=c, rot=rot, lcol=lcol, s=s: e.matmul(
                                pb[c][:], qTb[:, lcol:lcol + 128],
                                bwo[:, rot * 4096 + c * 512: rot * 4096 + (c + 1) * 512],
                                start=(s == 0), stop=(s == 7)),
                               waits=([wld] +
                                      ([("a", P.c("a")), ("v", P.c("v"))] if s == 0 else [])),
                               inc=(c == nb_ - 1))
                    wmm = P.c("t")
                    for c in range(nb_):
                        AE(lambda e, c=c, coff=coff: e.copy(
                            batn[:, coff + c * 512: coff + (c + 1) * 512], pb[c][:]),
                           waits=([("t", wmm)] + ([SMARK()] if c == 0 else [])),
                           inc=(c == nb_ - 1))
                ST(sc_attn[g, tt_], batn, waits=[("a", P.c("a"))])
            attn_mark[0] = SMARK()
        c_end_st = SMARK()

        braw = [AR1[:, 16384:17408], AR1[:, 17408:18432]]
        bvv = AR1[:, 18432:19456]
        btmp = AR1[:, 19456:20480]
        bacc = [AR1[:, 20480:21504], AR1[:, 21504:22528]]
        bdqa1r = [AR1[:, 0:4096], AR1[:, 4096:8192]]
        bdqa2r = [AR1[:, 8192:12288], AR1[:, 12288:16384]]
        bhD = AR1[:, 22528:23552]
        bsilD = AR1[:, 23552:24064]
        w3stg = AR1[:, 24064:24576]
        xnq = [AR2[:, 0:8192], AR2[:, 8192:16384]]
        hT_ready = {}
        d_boundary = [LMARK(), c_end_st, ("a", P.c("a")), ("v", P.c("v")), ("t", P.c("t"))]
        asm_t = [0, 0]
        asm_ready = [0, 0]
        xnh_t = [0, 0]
        w3q = list(range(32))

        def emit_dq_ht(ht):
            slot = ht % 2
            guard = [("t", asm_t[slot])] + (d_boundary if ht == 0 else [])
            dequant_job("w1", ht, dst_asm=bdqa1r[slot], asm_guard=guard)
            dequant_job("w2", ht, dst_asm=bdqa2r[slot], asm_guard=[("t", asm_t[slot])])
            asm_ready[slot] = P.c("a")

        emit_dq_ht(0)
        for ht in range(NHT):
            slot = ht % 2
            if ht + 1 < NHT:
                emit_dq_ht(ht + 1)
            if ht % 3 == 1 and w3q:
                dequant_job("w3", w3q.pop(0), stg=w3stg)
            wasm = asm_ready[slot]
            for half in range(2):
                for qtr in range(2):
                    rot = (half * 2 + qtr) % 2
                    qe = "sync" if rot == 0 else "gpsimd"
                    LD(xnq[rot], sc_xnT_own[half][:, qtr * 8192:(qtr + 1) * 8192],
                       waits=[("t", xnh_t[rot])] +
                             (d_boundary if ht == 0 and half == 0 and qtr == 0 else []),
                       qe=qe)
                    wldq = LMARK(qe)
                    for dsl in range(16):
                        ds = qtr * 16 + dsl
                        TE(lambda e, ds=ds, dsl=dsl, rot=rot, half=half, slot=slot: e.matmul(
                            pb[half][:], bdqa1r[slot][:, ds * 128:(ds + 1) * 128],
                            xnq[rot][:, dsl * 512:(dsl + 1) * 512],
                            start=(ds == 0), stop=(ds == 31)),
                           waits=([wldq, ("a", wasm)] +
                                  ([("a", P.c("a")), ("v", P.c("v"))] if ds == 0 else [])))
                        TE(lambda e, ds=ds, dsl=dsl, rot=rot, half=half, slot=slot: e.matmul(
                            pb[2 + half][:], bdqa2r[slot][:, ds * 128:(ds + 1) * 128],
                            xnq[rot][:, dsl * 512:(dsl + 1) * 512],
                            start=(ds == 0), stop=(ds == 31)),
                           inc=True)
                    xnh_t[rot] = P.c("t")
            asm_t[slot] = P.c("t")
            for half in range(2):
                wmmD = P.c("t")
                AE(lambda e, half=half: e.activation(bsilD[:], pb[half][:], AF.Silu),
                   waits=[("t", wmmD), ("v", P.c("v"))], inc=True)
                VE(lambda e, half=half: e.tensor_tensor(
                    out=bhD[:, half * 512:(half + 1) * 512], in0=bsilD[:], in1=pb[2 + half][:],
                    op=OP.mult),
                   waits=[("a", P.c("a"))] +
                         ([hT_ready[ht - 1]] if (half == 0 and ht > 0) else []), inc=True)
            ST(sc_hT[ht], bhD, waits=[("v", P.c("v"))])
            hT_ready[ht] = SMARK()
        while w3q:
            dequant_job("w3", w3q.pop(0), stg=w3stg)

        bhT = [AR1[:, 0:1024], AR1[:, 1024:2048]]
        bd3 = [AR1[:, 2048:2560], AR1[:, 2560:3072]]
        bep = [AR1[:, 3072 + k * 512: 3072 + (k + 1) * 512] for k in range(10)]
        bout = [AR1[:, 8192:8704], AR1[:, 8704:9216]]
        e_boundary = [LMARK(), ("a", P.c("a")), ("v", P.c("v"))]
        for d8 in range(8):
            for ht in range(NHT):
                rot = ht % 2
                qe = "sync" if rot == 0 else "gpsimd"
                LD(bhT[rot], sc_hT[ht], waits=[hT_ready[ht], ("t", P.c("t"))] +
                   (e_boundary if d8 == 0 and ht == 0 else []), qe=qe)
                for j in range(4):
                    LD(bd3[rot][:, j * 128:(j + 1) * 128], sc_dq3T[ht, d8 * 4 + j],
                       waits=([spill[("w3", d8 * 4 + 3)], ("t", P.c("t"))] if j == 0 else []), qe=qe)
                wld = LMARK(qe)
                for tt_ in range(8):
                    TE(lambda e, tt_=tt_, rot=rot, ht=ht: e.matmul(
                        pb[tt_][:], bhT[rot][:, tt_ * 128:(tt_ + 1) * 128], bd3[rot][:, 0:512],
                        start=(ht == 0), stop=(ht == NHT - 1)),
                       waits=([wld] +
                              ([("v", P.c("v")), ("a", P.c("a"))] if ht == 0 else [])),
                       inc=(tt_ == 7))
            wmm = P.c("t")
            for tt_ in range(8):
                rot = tt_ % 2
                qe = "sync" if rot == 0 else "gpsimd"
                LD(bep[5 * rot + 0], x_own[tt_ * 128:(tt_ + 1) * 128, d8 * 512:(d8 + 1) * 512],
                   waits=[("v", P.c("v"))], qe=qe)
                for g in range(NG):
                    LD(bep[5 * rot + 1 + g], sc_attn[g, tt_][:, d8 * 512:(d8 + 1) * 512],
                       waits=([c_end_st] if d8 == 0 and g == 0 else []), qe=qe)
                wld = LMARK(qe)
                VE(lambda e, tt_=tt_, rot=rot: e.tensor_tensor(
                    out=bout[rot], in0=pb[tt_][:], in1=bep[5 * rot + 0], op=OP.add),
                   waits=[wld, ("t", wmm), SMARK()])
                for g in range(NG):
                    VE(lambda e, g=g, rot=rot: e.tensor_tensor(
                        out=bout[rot], in0=bout[rot], in1=bep[5 * rot + 1 + g], op=OP.add),
                       inc=(g == NG - 1))
                wv = P.c("v")
                ST(out[tt_ * 128:(tt_ + 1) * 128, d8 * 512:(d8 + 1) * 512], bout[rot],
                   waits=[("v", wv)])
        P.emit("gpsimd", lambda e: e.nop(), [("st", P.c("st"))], False)

        def run(eng_name, proxy):
            own = P.esem[eng_name]
            self_serial = eng_name in ("vector", "scalar")
            idx = 0
            for waits, fn, isem, q in P.items[eng_name]:
                seen = {}
                if self_serial and idx > 0:
                    seen[own] = idx
                for s, val in waits:
                    if val > 0:
                        seen[s] = max(seen.get(s, 0), val)
                for s, val in seen.items():
                    proxy.wait_ge(sem[s], val)
                inst = fn(proxy)
                if isem is not None:
                    inst.then_inc(sem[isem], q)
                    idx += q

        @block.sync
        def _(sync):
            run("sync", sync)

        @block.tensor
        def _(tensor):
            run("tensor", tensor)

        @block.vector
        def _(vector):
            run("vector", vector)

        @block.scalar
        def _(scalar):
            run("scalar", scalar)

        @block.gpsimd
        def _(gpsimd):
            run("gpsimd", gpsimd)

    return nc


def host_inputs(x, Wq, Wk, Wv, Wo, w1, w2, w3):
    inv = 500000.0 ** (-(np.arange(0, 128, 2, dtype=np.float64) / 128.0))
    pos = np.arange(T, dtype=np.float64)
    fr = np.outer(pos, inv)
    c64, s64 = np.cos(fr), np.sin(fr)
    scl = 1.0 / np.sqrt(np.float64(128.0))

    def hilo(a64):
        hi = a64.astype(np.float32)
        lo = (a64 - hi.astype(np.float64)).astype(np.float32)
        return hi, lo

    ch, cl = hilo(c64)
    sh, sl = hilo(s64)
    rope_cov = np.ascontiguousarray(np.stack([ch, cl, sh, sl]))
    chq, clq = hilo(c64 * scl)
    shq, slq = hilo(s64 * scl)
    rope_q = np.stack([chq, clq, shq, slq])

    Wq_t = np.zeros((NG, 32, 128, 1024), np.float32)
    Wkv_t = np.zeros((NG, 32, 128, 512), np.float32)
    for g in range(NG):
        for ds in range(32):
            Wq_t[g, ds] = Wq[ds * 128:(ds + 1) * 128, g * 1024:(g + 1) * 1024]
            Wkv_t[g, ds, :, 0:256] = Wk[ds * 128:(ds + 1) * 128, g * 256:(g + 1) * 256]
            Wkv_t[g, ds, :, 256:512] = Wv[ds * 128:(ds + 1) * 128, g * 256:(g + 1) * 256]

    ident = np.eye(128, dtype=np.float32)
    ones = np.ones((128, 128), dtype=np.float32)
    maps = []
    for core in range(8):
        b, p1 = core // 4, core % 4
        p2 = 7 - p1
        xb = np.asarray(x[b])
        x_own_ = np.concatenate([xb[p1 * 512:(p1 + 1) * 512], xb[p2 * 512:(p2 + 1) * 512]])
        ro = np.concatenate([rope_q[:, p1 * 512:(p1 + 1) * 512],
                             rope_q[:, p2 * 512:(p2 + 1) * 512]], axis=1)
        m01_ = np.zeros((12, 4, 128, 512), np.float32)
        for jj in range(12):
            qc, kb = (0, jj) if jj < 4 else (1, jj - 4)
            posq = (p1 if qc == 0 else p2) * 512 + np.arange(512)
            for tks in range(4):
                posk = kb * 512 + tks * 128 + np.arange(128)
                m01_[jj, tks] = (posk[:, None] <= posq[None, :]).astype(np.float32)
        maps.append(dict(
            x_own=np.ascontiguousarray(x_own_), x_cov=np.ascontiguousarray(xb),
            Wq=Wq_t, Wkv=Wkv_t, Wo=np.ascontiguousarray(Wo),
            w1=np.ascontiguousarray(w1), w2=np.ascontiguousarray(w2),
            w3=np.ascontiguousarray(w3),
            rope_own=np.ascontiguousarray(ro.astype(np.float32)), rope_cov=rope_cov,
            m01=m01_, ident=ident, ones=ones))
    return maps


def kernel(L0_attn_x, L0_attn_Wq, L0_attn_Wk, L0_attn_Wv, L0_attn_Wo,
           L0_ffn_w1, L0_ffn_w2, L0_ffn_w3):
    x = np.asarray(L0_attn_x, dtype=np.float32)
    maps = host_inputs(x, np.asarray(L0_attn_Wq), np.asarray(L0_attn_Wk),
                       np.asarray(L0_attn_Wv), np.asarray(L0_attn_Wo),
                       np.asarray(L0_ffn_w1), np.asarray(L0_ffn_w2), np.asarray(L0_ffn_w3))
    nc = build_program()
    res = run_bass_kernel_spmd(nc, maps, core_ids=list(range(8)))
    outp = np.empty((B, T, D), dtype=np.float32)
    for core in range(8):
        b, p1 = core // 4, core % 4
        p2 = 7 - p1
        o = res.results[core]["out"]
        outp[b, p1 * 512:(p1 + 1) * 512] = o[0:512]
        outp[b, p2 * 512:(p2 + 1) * 512] = o[512:1024]
    return outp


# version 07
# speedup vs baseline: 1.1564x; 1.1564x over previous
import math
import numpy as np
import contextlib
import concourse.bass as bass
import concourse.mybir as mybir
from concourse.bass_utils import run_bass_kernel_spmd
from concourse.alu_op_type import AluOpType

F32 = mybir.dt.float32
AF = mybir.ActivationFunctionType
AX = mybir.AxisListType
OP = AluOpType

B, T, D, HFF = 2, 4096, 4096, 11008
NG = 4
TOWN = 1024
NHT = HFF // 128
EPS = 1e-5

CODE64 = np.array([-1.0, -0.6961928009986877, -0.5250730514526367, -0.39491748809814453,
                   -0.28444138169288635, -0.18477343022823334, -0.09105003625154495, 0.0,
                   0.07958029955625534, 0.16093020141124725, 0.24611230194568634,
                   0.33791524171829224, 0.44070982933044434, 0.5626170039176941,
                   0.7229568362236023, 1.0])
B64 = (CODE64[:-1] + CODE64[1:]) * 0.5
BHI = B64.astype(np.float32)
BLO = (B64 - BHI.astype(np.float64)).astype(np.float32)
DLT = np.diff(CODE64).astype(np.float32)
C0 = float(np.float32(CODE64[0]))

ENGS = ("sync", "tensor", "vector", "scalar", "gpsimd")


class Plan:
    def __init__(self):
        self.items = {e: [] for e in ENGS}
        self.cnt = {"ld": 0, "st": 0, "t": 0, "v": 0, "a": 0, "sl": 0}
        self.esem = {"sync": "ld", "gpsimd": "st", "tensor": "t", "vector": "v", "scalar": "a"}

    def emit(self, eng, fn, waits=(), inc=False, dma=False):
        q = 16 if (dma or eng in ("sync", "gpsimd")) else 1
        sem = self.esem[eng]
        if dma and eng == "scalar":
            sem = "sl"
            inc = True
        elif eng in ("vector", "scalar"):
            inc = True
        self.items[eng].append((list(waits), fn, sem if inc else None, q))
        if inc:
            self.cnt[sem] += q

    def c(self, s):
        return self.cnt[s]


def build_program():
    nc = bass.Bass("TRN2", target_bir_lowering=False)
    dt = nc.dram_tensor
    x_own = dt("x_own", [TOWN, D], F32, kind="ExternalInput").ap()
    x_cov = dt("x_cov", [T, D], F32, kind="ExternalInput").ap()
    Wq = dt("Wq", [NG, 32, 128, 1024], F32, kind="ExternalInput").ap()
    Wkv = dt("Wkv", [NG, 32, 128, 512], F32, kind="ExternalInput").ap()
    Wo = dt("Wo", [D, D], F32, kind="ExternalInput").ap()
    w1 = dt("w1", [HFF, D], F32, kind="ExternalInput").ap()
    w2 = dt("w2", [HFF, D], F32, kind="ExternalInput").ap()
    w3 = dt("w3", [D, HFF], F32, kind="ExternalInput").ap()
    rope_own = dt("rope_own", [4, TOWN, 64], F32, kind="ExternalInput").ap()
    rope_cov = dt("rope_cov", [4, T, 64], F32, kind="ExternalInput").ap()
    m01 = dt("m01", [12, 4, 128, 512], F32, kind="ExternalInput").ap()
    ident_in = dt("ident", [128, 128], F32, kind="ExternalInput").ap()
    ones_in = dt("ones", [128, 128], F32, kind="ExternalInput").ap()
    out = dt("out", [TOWN, D], F32, kind="ExternalOutput").ap()

    sc_xnT_cov = dt("sc_xnT_cov", [8, 128, 16384], F32).ap()
    sc_xnT_own = dt("sc_xnT_own", [2, 128, 16384], F32).ap()
    sc_dq3T = dt("sc_dq3T", [NHT, 32, 128, 128], F32).ap()
    sc_hT = dt("sc_hT", [NHT, 128, 1024], F32).ap()
    sc_attn = dt("sc_attn", [NG, 8, 128, 4096], F32).ap()

    P = Plan()
    with contextlib.ExitStack() as es:
        sbuf = lambda name, cols: es.enter_context(nc.sbuf_tensor(name, [128, cols], F32))
        AR1 = sbuf("AR1", 24576)
        AR2 = sbuf("AR2", 16384)
        AR3 = sbuf("AR3", 5120)
        SM = sbuf("SM", 192)
        RPO = sbuf("RPO", 256)
        RPC = sbuf("RPC", 256)
        IDT = sbuf("IDT", 128)
        ONE = sbuf("ONE", 128)
        SUMS = sbuf("SUMS", 2048)
        pb = [es.enter_context(nc.psum_tensor("pb%d" % i, [128, 512], F32)) for i in range(8)]
        sem = {k: es.enter_context(nc.semaphore("sem_" + k))
               for k in ("ld", "st", "t", "v", "a", "sl")}
        block = es.enter_context(nc.Block())

        nc.const_aps.register(F32, EPS, SM[:, 8:9])

        def LD(dst, src, waits=(), qe="sync"):
            P.emit(qe, lambda e, d=dst, s=src: e.dma_start(out=d, in_=s), waits, True, dma=True)

        def ST(dst, src, waits=()):
            P.emit("gpsimd", lambda e, d=dst, s=src: e.dma_start(out=d, in_=s), waits, True)

        def LMARK(qe="sync"):
            sm = {"sync": "ld", "gpsimd": "st", "scalar": "sl"}[qe]
            P.emit(qe, lambda e: e.nop(), [(sm, P.c(sm))], False)
            return (sm, P.c(sm))

        def SMARK():
            P.emit("gpsimd", lambda e: e.nop(), [("st", P.c("st"))], False)
            return ("st", P.c("st"))

        TE = lambda fn, waits=(), inc=False: P.emit("tensor", fn, waits, inc)
        VE = lambda fn, waits=(), inc=False: P.emit("vector", fn, waits, inc)
        AE = lambda fn, waits=(), inc=False: P.emit("scalar", fn, waits, inc)

        VE(lambda e: e.memset(SM[:, 8:9], EPS))
        LD(IDT[:], ident_in)
        LD(ONE[:], ones_in)
        w_consts = [LMARK(), ("v", P.c("v"))]

        bx = AR1[:, 0:4096]
        bsq = AR1[:, 4096:8192]
        bxn = AR1[:, 8192:12288]
        basm = AR2[:, 0:16384]
        s_sum, s_sq = SM[:, 0:1], SM[:, 1:2]
        s_m, s_ex2 = SM[:, 2:3], SM[:, 3:4]
        s_mm, s_var = SM[:, 4:5], SM[:, 5:6]
        s_sd, s_r = SM[:, 6:7], SM[:, 7:8]
        s_mr, s_b = SM[:, 10:11], SM[:, 11:12]

        blocks = [("cov", kb) for kb in range(8)] + [("own", h) for h in range(2)]
        xnT_ready = {}
        for bid, (kind, idx) in enumerate(blocks):
            src = x_cov if kind == "cov" else x_own
            for ts_ in range(4):
                row0 = idx * 512 + ts_ * 128
                qe = "sync" if ts_ % 2 == 0 else "gpsimd"
                LD(bx, src[row0:row0 + 128, :],
                   waits=[("v", P.c("v")), ("a", P.c("a"))] + (w_consts if bid == 0 and ts_ == 0 else []),
                   qe=qe)
                wld = LMARK(qe)
                VE(lambda e: e.tensor_reduce(out=s_sum, in_=bx, axis=AX.X, op=OP.add),
                   waits=[wld])
                AE(lambda e: e.activation(bsq, bx, AF.Square),
                   waits=[wld, ("v", P.c("v"))], inc=True)
                VE(lambda e: e.tensor_reduce(out=s_sq, in_=bsq, axis=AX.X, op=OP.add),
                   waits=[("a", P.c("a"))])
                VE(lambda e: e.tensor_scalar_mul(s_m, s_sum, 1.0 / D))
                VE(lambda e: e.tensor_scalar_mul(s_ex2, s_sq, 1.0 / D))
                VE(lambda e: e.tensor_tensor(out=s_mm, in0=s_m, in1=s_m, op=OP.mult))
                VE(lambda e: e.tensor_tensor(out=s_var, in0=s_ex2, in1=s_mm, op=OP.subtract), inc=True)
                AE(lambda e: e.activation(s_sd, s_var, AF.Sqrt, bias=EPS),
                   waits=[("v", P.c("v"))], inc=True)
                VE(lambda e: e.reciprocal(s_r, s_sd), waits=[("a", P.c("a"))])
                VE(lambda e: e.tensor_tensor(out=s_mr, in0=s_m, in1=s_r, op=OP.mult))
                VE(lambda e: e.tensor_scalar_mul(s_b, s_mr, -1.0), inc=True)
                AE(lambda e: e.activation(bxn, bx, AF.Identity, bias=s_b, scale=s_r),
                   waits=[("v", P.c("v"))], inc=True)
                for grp in range(8):
                    for j in range(4):
                        ds = grp * 4 + j
                        TE(lambda e, j=j, ds=ds: e.transpose(
                            pb[6][:, j * 128:(j + 1) * 128], bxn[:, ds * 128:(ds + 1) * 128], IDT[:]),
                           waits=([("a", P.c("a"))] if j == 0 else []), inc=(j == 3))
                    wt = P.c("t")
                    for j in range(4):
                        ds = grp * 4 + j
                        cc = ds * 512 + ts_ * 128
                        AE(lambda e, j=j, cc=cc: e.copy(
                            basm[:, cc:cc + 128], pb[6][:, j * 128:(j + 1) * 128]),
                           waits=[("t", wt)], inc=(j == 3))
            dstb = (sc_xnT_cov if kind == "cov" else sc_xnT_own)[idx]
            ST(dstb, basm, waits=[("a", P.c("a"))])
            xnT_ready[(kind, idx)] = SMARK()
            if bid + 1 < len(blocks):
                AE(lambda e: e.nop(), waits=[xnT_ready[(kind, idx)]], inc=True)

        bwaits = [LMARK(), SMARK()] + [(k, P.c(k)) for k in ("t", "v", "a")]
        P.emit("sync", lambda e: e.nop(), [w for w in bwaits if w[0] != "lg"], False)
        P.emit("gpsimd", lambda e: e.nop(), [w for w in bwaits if w[0] != "sg"], False)
        P.emit("tensor", lambda e: e.nop(), [w for w in bwaits if w[0] != "t"], True)
        P.emit("vector", lambda e: e.nop(), [w for w in bwaits if w[0] != "v"], True)
        P.emit("scalar", lambda e: e.nop(), [w for w in bwaits if w[0] != "a"], True)

        bprobs = AR1[:, 8192:10240]
        m1b = bprobs[:, 0:512]
        m2b = bprobs[:, 512:1024]
        o1b = bprobs[:, 1024:1536]
        o2b = bprobs[:, 1536:2048]
        braw = AR1[:, 10240:12288]
        bvv = AR1[:, 12288:14336]
        btmp = AR1[:, 14336:16384]
        bacc = [AR1[:, 16384:18432], AR1[:, 18432:20480]]
        bdqa = AR1[:, 20480:24576]
        s_am = SM[:, 64:128]
        s_rc = SM[:, 128:192]
        qTb = AR1[:, 0:8192]
        kTb = AR2[:, 0:8192]
        vTb = AR2[:, 8192:16384]
        bwo = AR2[:, 0:8192]
        batn = AR2[:, 8192:12288]
        xa = [AR3[:, 0:512], AR3[:, 512:1024]]
        xb = [AR3[:, 1024:1536], AR3[:, 1536:2048]]
        wpc = [AR3[:, 2048:3072], AR3[:, 3072:4096]]
        bqro = AR3[:, 4096:4608]
        bmsk = AR3[:, 4608:5120]
        rop = [RPO[:, i * 64:(i + 1) * 64] for i in range(4)]
        rcv = [RPC[:, i * 64:(i + 1) * 64] for i in range(4)]
        attn_mark = [None]

        spill = {}
        dq_state = {"wt_prev": 0}

        dqst = {"cons": [0, 0], "slot": 0, "stg": 0}

        def dequant_job(tag, idx, dst_asm=None, asm_guard=(), stg=None):
            if tag == "w1":
                src = w1
                chunks = [(0, 2048), (2048, 2048)]
            elif tag == "w2":
                src = w2
                chunks = [(0, 2048), (2048, 2048)]
            else:
                src = w3
                chunks = [(k * 2048, 2048) for k in range(5)] + [(10240, 768)]
            first_chunk = True
            for (c0, cw) in chunks:
                nb = cw // 64
                slot = dqst["slot"]
                dqst["slot"] = 1 - slot
                brw = braw[slot]
                LD(brw[:, 0:cw], src[idx * 128:(idx + 1) * 128, c0:c0 + cw],
                   waits=[("v", dqst["cons"][slot])], qe="scalar")
                wld = LMARK("scalar")
                rawv = brw[:, 0:cw].rearrange("p (b j) -> p b j", j=64)
                VE(lambda e, rawv=rawv, nb=nb: e.tensor_reduce(
                    out=s_am[:, 0:nb], in_=rawv, axis=AX.X, op=OP.max, apply_absolute_value=True),
                   waits=[wld])
                VE(lambda e, nb=nb: e.reciprocal(s_rc[:, 0:nb], s_am[:, 0:nb]))
                rbc = s_rc[:, 0:nb].unsqueeze(2).broadcast_to((128, nb, 64))
                VE(lambda e, rawv=rawv, rbc=rbc, cw=cw: e.tensor_tensor(
                    out=bvv[:, 0:cw].rearrange("p (b j) -> p b j", j=64), in0=rawv, in1=rbc,
                    op=OP.mult), inc=True)
                dqst["cons"][slot] = P.c("v")
                VE(lambda e, cw=cw: e.tensor_scalar(out=btmp[:, 0:cw], in0=bvv[:, 0:cw],
                                                    scalar1=float(BHI[0]), scalar2=float(BLO[0]),
                                                    op0=OP.subtract, op1=OP.is_gt))
                VE(lambda e, cw=cw: e.tensor_scalar_mul(bacc[0][:, 0:cw], btmp[:, 0:cw],
                                                        float(DLT[0])),
                   waits=[("t", dq_state["wt_prev"])])
                a = 0
                for i in range(1, 15):
                    VE(lambda e, cw=cw, i=i: e.tensor_scalar(
                        out=btmp[:, 0:cw], in0=bvv[:, 0:cw], scalar1=float(BHI[i]),
                        scalar2=float(BLO[i]), op0=OP.subtract, op1=OP.is_gt))
                    VE(lambda e, cw=cw, i=i, a=a: e.scalar_tensor_tensor(
                        out=bacc[1 - a][:, 0:cw], in0=btmp[:, 0:cw], scalar=float(DLT[i]),
                        in1=bacc[a][:, 0:cw], op0=OP.mult, op1=OP.add))
                    a = 1 - a
                abc = s_am[:, 0:nb].unsqueeze(2).broadcast_to((128, nb, 64))
                VE(lambda e, cw=cw, a=a, abc=abc: e.scalar_tensor_tensor(
                    out=bacc[1 - a][:, 0:cw].rearrange("p (b j) -> p b j", j=64),
                    in0=bacc[a][:, 0:cw].rearrange("p (b j) -> p b j", j=64),
                    scalar=C0, in1=abc, op0=OP.add, op1=OP.mult), inc=True)
                wdq = P.c("v")
                dqbuf = bacc[1 - a]
                npc = cw // 128
                for p0 in range(0, npc, 4):
                    n4 = min(4, npc - p0)
                    for j in range(n4):
                        TE(lambda e, j=j, dqbuf=dqbuf, p0=p0: e.transpose(
                            pb[7][:, j * 128:(j + 1) * 128],
                            dqbuf[:, (p0 + j) * 128:(p0 + j + 1) * 128], IDT[:]),
                           waits=([("v", wdq), ("a", P.c("a"))] if j == 0 else []),
                           inc=(j == n4 - 1))
                    wt = P.c("t")
                    ev_waits = [("t", wt)]
                    if tag != "w3":
                        if first_chunk and p0 == 0:
                            ev_waits.extend(asm_guard)
                        for j in range(n4):
                            gp = c0 // 128 + p0 + j
                            AE(lambda e, j=j, gp=gp, dst_asm=dst_asm: e.copy(
                                dst_asm[:, gp * 128:(gp + 1) * 128], pb[7][:, j * 128:(j + 1) * 128]),
                               waits=(ev_waits if j == 0 else []), inc=(j == n4 - 1))
                    else:
                        ev_waits.append(SMARK())
                        for j in range(n4):
                            AE(lambda e, j=j, stg=stg: e.copy(
                                stg[:, j * 128:(j + 1) * 128], pb[7][:, j * 128:(j + 1) * 128]),
                               waits=(ev_waits if j == 0 else []), inc=(j == n4 - 1))
                        wa = P.c("a")
                        for j in range(n4):
                            ht = c0 // 128 + p0 + j
                            ST(sc_dq3T[ht, idx], stg[:, j * 128:(j + 1) * 128], waits=[("a", wa)])
                dq_state["wt_prev"] = P.c("t")
                first_chunk = False
            if tag == "w3":
                spill[(tag, idx)] = SMARK()



        def rope_apply(mk, nh, tabs, dst, dstoff, waits0):
            ch, cl, sh, sl = tabs
            bc = lambda t: t.unsqueeze(1).broadcast_to((128, nh, 64))
            x1 = mk(0, 64)
            x2 = mk(64, 128)
            dview = dst[:, dstoff:dstoff + nh * 128].rearrange("p (h j) -> p h j", j=128)
            dst1 = dview[:, :, 0:64]
            dst2 = dview[:, :, 64:128]
            mm1 = m1b[:, 0:nh * 64].rearrange("p (h j) -> p h j", j=64)
            mm2 = m2b[:, 0:nh * 64].rearrange("p (h j) -> p h j", j=64)
            oo1 = o1b[:, 0:nh * 64].rearrange("p (h j) -> p h j", j=64)
            oo2 = o2b[:, 0:nh * 64].rearrange("p (h j) -> p h j", j=64)
            VE(lambda e: e.tensor_tensor(out=mm1, in0=x1, in1=bc(ch), op=OP.mult), waits=waits0)
            VE(lambda e: e.tensor_tensor(out=mm2, in0=x2, in1=bc(sh), op=OP.mult))
            VE(lambda e: e.tensor_tensor(out=oo1, in0=mm1, in1=mm2, op=OP.subtract))
            VE(lambda e: e.tensor_tensor(out=mm1, in0=x1, in1=bc(cl), op=OP.mult))
            VE(lambda e: e.tensor_tensor(out=mm2, in0=x2, in1=bc(sl), op=OP.mult))
            VE(lambda e: e.tensor_tensor(out=oo2, in0=mm1, in1=mm2, op=OP.subtract))
            VE(lambda e: e.tensor_tensor(out=dst1, in0=oo1, in1=oo2, op=OP.add))
            VE(lambda e: e.tensor_tensor(out=mm1, in0=x1, in1=bc(sh), op=OP.mult))
            VE(lambda e: e.tensor_tensor(out=mm2, in0=x2, in1=bc(ch), op=OP.mult))
            VE(lambda e: e.tensor_tensor(out=oo1, in0=mm1, in1=mm2, op=OP.add))
            VE(lambda e: e.tensor_tensor(out=mm1, in0=x1, in1=bc(sl), op=OP.mult))
            VE(lambda e: e.tensor_tensor(out=mm2, in0=x2, in1=bc(cl), op=OP.mult))
            VE(lambda e: e.tensor_tensor(out=oo2, in0=mm1, in1=mm2, op=OP.add))
            VE(lambda e: e.tensor_tensor(out=dst2, in0=oo1, in1=oo2, op=OP.add), inc=True)

        for g in range(NG):
            for pr in range(4):
                for ds in range(32):
                    rot = ds % 2
                    qe = "sync" if rot == 0 else "gpsimd"
                    LD(wpc[rot][:, 0:1024], Wq[g, ds], waits=[("t", P.c("t"))], qe=qe)
                    for ti in range(2):
                        tile = pr * 2 + ti
                        half, tsub = tile // 4, tile % 4
                        cc = ds * 512 + tsub * 128
                        LD((xa if ti == 0 else xb)[rot][:, 0:128],
                           sc_xnT_own[half][:, cc:cc + 128],
                           waits=[xnT_ready[("own", half)], ("t", P.c("t"))], qe=qe)
                    wld = LMARK(qe)
                    for ti in range(2):
                        xsrc = (xa if ti == 0 else xb)[rot][:, 0:128]
                        for nbk in range(2):
                            TE(lambda e, ti=ti, nbk=nbk, xsrc=xsrc, rot=rot, ds=ds: e.matmul(
                                pb[2 * ti + nbk][:], xsrc,
                                wpc[rot][:, nbk * 512:(nbk + 1) * 512],
                                start=(ds == 0), stop=(ds == 31)),
                               waits=([wld] +
                                      ([("v", P.c("v")), ("a", P.c("a"))] if ds == 0 else [])),
                               inc=(ti == 1 and nbk == 1))
                wmm = P.c("t")
                for ti in range(2):
                    tile = pr * 2 + ti
                    qe = "sync" if ti % 2 == 0 else "gpsimd"
                    for i in range(4):
                        LD(rop[i], rope_own[i, tile * 128:(tile + 1) * 128, :],
                           waits=[("v", P.c("v"))], qe=qe)
                    wld2 = LMARK(qe)
                    for bk in range(2):
                        bank = pb[2 * ti + bk]
                        mk = lambda lo, hi, bank=bank: bank[:, :].rearrange(
                            "p (h j) -> p h j", j=128)[:, :, lo:hi]
                        rope_apply(mk, 4, rop, bqro, 0,
                                   [wld2, ("t", wmm), ("a", P.c("a"))])
                        wv = P.c("v")
                        for j in range(4):
                            TE(lambda e, j=j: e.transpose(
                                pb[4][:, j * 128:(j + 1) * 128], bqro[:, j * 128:(j + 1) * 128],
                                IDT[:]),
                               waits=([("v", wv), ("a", P.c("a"))] if j == 0 else []),
                               inc=(j == 3))
                        wt = P.c("t")
                        for j in range(4):
                            h = bk * 4 + j
                            c1 = h * 1024 + tile * 128
                            AE(lambda e, j=j, c1=c1: e.copy(
                                qTb[:, c1:c1 + 128], pb[4][:, j * 128:(j + 1) * 128]),
                               waits=[("t", wt)], inc=(j == 3))
            for kb in range(8):
                for ds in range(32):
                    rot = ds % 2
                    qe = "sync" if rot == 0 else "gpsimd"
                    LD(wpc[rot][:, 0:512], Wkv[g, ds], waits=[("t", P.c("t"))], qe=qe)
                    LD(xa[rot][:, 0:512], sc_xnT_cov[kb][:, ds * 512:ds * 512 + 512],
                       waits=[xnT_ready[("cov", kb)], ("t", P.c("t"))], qe=qe)
                    wld = LMARK(qe)
                    for tks in range(4):
                        TE(lambda e, tks=tks, rot=rot, ds=ds: e.matmul(
                            pb[tks][:], xa[rot][:, tks * 128:(tks + 1) * 128],
                            wpc[rot][:, 0:512], start=(ds == 0), stop=(ds == 31)),
                           waits=([wld] +
                                  ([("v", P.c("v")), ("a", P.c("a"))] if ds == 0 else [])),
                           inc=(tks == 3))
                wmm = P.c("t")
                for tks in range(4):
                    qe = "sync" if tks % 2 == 0 else "gpsimd"
                    for i in range(4):
                        LD(rcv[i], rope_cov[i, kb * 512 + tks * 128: kb * 512 + (tks + 1) * 128, :],
                           waits=[("v", P.c("v"))], qe=qe)
                    wld2 = LMARK(qe)
                    bank = pb[tks]
                    mk = lambda lo, hi, bank=bank: bank[:, 0:256].rearrange(
                        "p (h j) -> p h j", j=128)[:, :, lo:hi]
                    rope_apply(mk, 2, rcv, bqro, 0, [wld2, ("t", wmm), ("a", P.c("a"))])
                    wv = P.c("v")
                    for j in range(2):
                        TE(lambda e, j=j: e.transpose(
                            pb[4][:, j * 128:(j + 1) * 128], bqro[:, j * 128:(j + 1) * 128], IDT[:]),
                           waits=([("v", wv), ("a", P.c("a"))] if j == 0 else []), inc=(j == 1))
                    wt = P.c("t")
                    for j in range(2):
                        c1 = j * 4096 + kb * 512 + tks * 128
                        AE(lambda e, j=j, c1=c1: e.copy(
                            kTb[:, c1:c1 + 128], pb[4][:, j * 128:(j + 1) * 128]),
                           waits=[("t", wt)] +
                                 ([attn_mark[0]] if (kb == 0 and tks == 0 and j == 0
                                                     and attn_mark[0]) else []),
                           inc=(j == 1))
                    for j in range(2):
                        c1 = ((kb * 4 + tks) * 2 + j) * 128
                        AE(lambda e, j=j, c1=c1, tks=tks: e.copy(
                            vTb[:, c1:c1 + 128], pb[tks][:, 256 + j * 128:256 + (j + 1) * 128]),
                           inc=(j == 1))
            bank_exp = [0, 0]
            last_mult_v = 0
            last_avsums_t = 0
            for h in range(8):
                kvh = h // 4
                hfirst = [("a", P.c("a")), ("v", P.c("v"))]
                for qc in range(2):
                    nkb = 4 if qc == 0 else 8
                    nsteps = nkb * 4
                    step = 0
                    for kb in range(nkb):
                        blkid = kb if qc == 0 else 4 + kb
                        for tks in range(4):
                            first = (step == 0)
                            last = (step == nsteps - 1)
                            sbank = pb[step % 2]
                            kcol = kvh * 4096 + kb * 512 + tks * 128
                            qcol = h * 1024 + qc * 512
                            TE(lambda e, sbank=sbank, kcol=kcol, qcol=qcol: e.matmul(
                                sbank[:], kTb[:, kcol:kcol + 128], qTb[:, qcol:qcol + 512],
                                start=True, stop=True),
                               waits=[("a", bank_exp[step % 2])] +
                                     (hfirst if (first and qc == 0) else []),
                               inc=True)
                            wsc = P.c("t")
                            AE(lambda e, sbank=sbank: e.activation(
                                bprobs[:, 0:512], sbank[:], AF.Exp),
                               waits=[("t", wsc), ("v", last_mult_v)], inc=True)
                            wex = P.c("a")
                            bank_exp[step % 2] = wex
                            qe = "sync" if step % 2 == 0 else "gpsimd"
                            LD(bmsk, m01[blkid, tks], waits=[("v", last_mult_v)], qe=qe)
                            wmk = LMARK(qe)
                            VE(lambda e: e.tensor_tensor(
                                out=bqro[:], in0=bprobs[:, 0:512], in1=bmsk[:], op=OP.mult),
                               waits=[("a", wex), wmk, ("t", last_avsums_t)], inc=True)
                            wpm = P.c("v")
                            last_mult_v = wpm
                            vcol = ((kb * 4 + tks) * 2 + kvh) * 128
                            TE(lambda e, vcol=vcol, qc=qc, first=first, last=last: e.matmul(
                                pb[2 + qc][:], vTb[:, vcol:vcol + 128], bqro[:],
                                start=first, stop=last),
                               waits=[("v", wpm)], inc=True)
                            TE(lambda e, qc=qc, first=first, last=last: e.matmul(
                                pb[4 + qc][0:1, :], ONE[:, 0:1], bqro[:],
                                start=first, stop=last),
                               inc=True)
                            last_avsums_t = P.c("t")
                            step += 1
                wavd = P.c("t")
                AE(lambda e: e.copy(SUMS[0:1, 0:512], pb[4][0:1, :]), waits=[("t", wavd)])
                AE(lambda e: e.copy(SUMS[0:1, 512:1024], pb[5][0:1, :]), inc=True)
                VE(lambda e: e.reciprocal(SUMS[0:1, 1024:2048], SUMS[0:1, 0:1024]),
                   waits=[("a", P.c("a"))], inc=True)
                wrs = P.c("v")
                for qc in range(2):
                    TE(lambda e, qc=qc: e.matmul(
                        pb[qc][:], ONE[0:1, 0:128], SUMS[0:1, 1024 + qc * 512:1024 + (qc + 1) * 512],
                        start=True, stop=True),
                       waits=([("v", wrs), ("a", P.c("a"))] if qc == 0 else []), inc=(qc == 1))
                wbc = P.c("t")
                for qc in range(2):
                    AE(lambda e, qc=qc: e.copy(bprobs[:, qc * 512:(qc + 1) * 512], pb[qc][:]),
                       waits=([("t", wbc), ("v", P.c("v"))] if qc == 0 else []), inc=(qc == 1))
                wev = P.c("a")
                for qc in range(2):
                    c1 = h * 1024 + qc * 512
                    VE(lambda e, qc=qc, c1=c1: e.tensor_tensor(
                        out=qTb[:, c1:c1 + 512], in0=pb[2 + qc][:],
                        in1=bprobs[:, qc * 512:(qc + 1) * 512], op=OP.mult),
                       waits=[("a", wev)], inc=(qc == 1))
            for tt_ in range(8):
                for rnd, (nb_, coff) in enumerate([(6, 0), (2, 3072)]):
                    for s in range(8):
                        rot = s % 2
                        qe = "sync" if rot == 0 else "gpsimd"
                        LD(bwo[:, rot * 4096: rot * 4096 + nb_ * 512],
                           Wo[g * 1024 + s * 128: g * 1024 + s * 128 + 128, coff:coff + nb_ * 512],
                           waits=[("t", P.c("t"))] +
                                 ([("v", P.c("v"))] if tt_ == 0 and rnd == 0 and s == 0 else []),
                           qe=qe)
                        wld = LMARK(qe)
                        for c in range(nb_):
                            lcol = (g * 0 + s) * 1024 + tt_ * 128
                            TE(lambda e, c=c, rot=rot, lcol=lcol, s=s: e.matmul(
                                pb[c][:], qTb[:, lcol:lcol + 128],
                                bwo[:, rot * 4096 + c * 512: rot * 4096 + (c + 1) * 512],
                                start=(s == 0), stop=(s == 7)),
                               waits=([wld] +
                                      ([("a", P.c("a")), ("v", P.c("v"))] if s == 0 else [])),
                               inc=(c == nb_ - 1))
                    wmm = P.c("t")
                    for c in range(nb_):
                        AE(lambda e, c=c, coff=coff: e.copy(
                            batn[:, coff + c * 512: coff + (c + 1) * 512], pb[c][:]),
                           waits=([("t", wmm)] + ([SMARK()] if c == 0 else [])),
                           inc=(c == nb_ - 1))
                ST(sc_attn[g, tt_], batn, waits=[("a", P.c("a"))])
            attn_mark[0] = SMARK()
        c_end_st = SMARK()

        braw = [AR1[:, 8192:10240], AR1[:, 10240:12288]]
        bvv = AR1[:, 12288:14336]
        btmp = AR1[:, 14336:16384]
        bacc = [AR1[:, 16384:18432], AR1[:, 18432:20480]]
        bdqa1r = [AR1[:, 0:4096], AR1[:, 4096:8192]]
        bdqa2r = [AR2[:, 8192:12288], AR2[:, 12288:16384]]
        bhD = AR1[:, 20480:21504]
        bsilD = AR1[:, 21504:22016]
        w3stg = AR1[:, 22016:22528]
        xnq = [AR2[:, 0:4096], AR2[:, 4096:8192]]
        hT_ready = {}
        d_boundary = [LMARK(), c_end_st, ("a", P.c("a")), ("v", P.c("v")), ("t", P.c("t"))]
        asm_t = [0, 0]
        asm_ready = [0, 0]
        xnh_t = [0, 0]
        w3q = list(range(32))

        def emit_dq_ht(ht):
            slot = ht % 2
            guard = [("t", asm_t[slot])] + (d_boundary if ht == 0 else [])
            dequant_job("w1", ht, dst_asm=bdqa1r[slot], asm_guard=guard)
            dequant_job("w2", ht, dst_asm=bdqa2r[slot], asm_guard=[("t", asm_t[slot])])
            asm_ready[slot] = P.c("a")

        emit_dq_ht(0)
        for ht in range(NHT):
            slot = ht % 2
            if ht + 1 < NHT:
                emit_dq_ht(ht + 1)
            if ht % 3 == 1 and w3q:
                dequant_job("w3", w3q.pop(0), stg=w3stg)
            wasm = asm_ready[slot]
            for half in range(2):
                for qtr in range(4):
                    rot = qtr % 2
                    qe = "sync" if rot == 0 else "gpsimd"
                    LD(xnq[rot], sc_xnT_own[half][:, qtr * 4096:(qtr + 1) * 4096],
                       waits=[("t", xnh_t[rot])] +
                             (d_boundary if ht == 0 and half == 0 and qtr == 0 else []),
                       qe=qe)
                    wldq = LMARK(qe)
                    for dsl in range(8):
                        ds = qtr * 8 + dsl
                        TE(lambda e, ds=ds, dsl=dsl, rot=rot, half=half, slot=slot: e.matmul(
                            pb[half][:], bdqa1r[slot][:, ds * 128:(ds + 1) * 128],
                            xnq[rot][:, dsl * 512:(dsl + 1) * 512],
                            start=(ds == 0), stop=(ds == 31)),
                           waits=([wldq, ("a", wasm)] +
                                  ([("a", P.c("a")), ("v", P.c("v"))] if ds == 0 else [])))
                        TE(lambda e, ds=ds, dsl=dsl, rot=rot, half=half, slot=slot: e.matmul(
                            pb[2 + half][:], bdqa2r[slot][:, ds * 128:(ds + 1) * 128],
                            xnq[rot][:, dsl * 512:(dsl + 1) * 512],
                            start=(ds == 0), stop=(ds == 31)),
                           inc=True)
                    xnh_t[rot] = P.c("t")
            asm_t[slot] = P.c("t")
            for half in range(2):
                wmmD = P.c("t")
                AE(lambda e, half=half: e.activation(bsilD[:], pb[half][:], AF.Silu),
                   waits=[("t", wmmD), ("v", P.c("v"))], inc=True)
                VE(lambda e, half=half: e.tensor_tensor(
                    out=bhD[:, half * 512:(half + 1) * 512], in0=bsilD[:], in1=pb[2 + half][:],
                    op=OP.mult),
                   waits=[("a", P.c("a"))] +
                         ([hT_ready[ht - 1]] if (half == 0 and ht > 0) else []), inc=True)
            ST(sc_hT[ht], bhD, waits=[("v", P.c("v"))])
            hT_ready[ht] = SMARK()
        while w3q:
            dequant_job("w3", w3q.pop(0), stg=w3stg)

        bhT = [AR1[:, 0:1024], AR1[:, 1024:2048]]
        bd3 = [AR1[:, 2048:2560], AR1[:, 2560:3072]]
        bep = [AR1[:, 3072 + k * 512: 3072 + (k + 1) * 512] for k in range(10)]
        bout = [AR1[:, 8192:8704], AR1[:, 8704:9216]]
        e_boundary = [LMARK(), ("a", P.c("a")), ("v", P.c("v"))]
        for d8 in range(8):
            for ht in range(NHT):
                rot = ht % 2
                qe = "sync" if rot == 0 else "gpsimd"
                LD(bhT[rot], sc_hT[ht], waits=[hT_ready[ht], ("t", P.c("t"))] +
                   (e_boundary if d8 == 0 and ht == 0 else []), qe=qe)
                for j in range(4):
                    LD(bd3[rot][:, j * 128:(j + 1) * 128], sc_dq3T[ht, d8 * 4 + j],
                       waits=([spill[("w3", d8 * 4 + 3)], ("t", P.c("t"))] if j == 0 else []), qe=qe)
                wld = LMARK(qe)
                for tt_ in range(8):
                    TE(lambda e, tt_=tt_, rot=rot, ht=ht: e.matmul(
                        pb[tt_][:], bhT[rot][:, tt_ * 128:(tt_ + 1) * 128], bd3[rot][:, 0:512],
                        start=(ht == 0), stop=(ht == NHT - 1)),
                       waits=([wld] +
                              ([("v", P.c("v")), ("a", P.c("a"))] if ht == 0 else [])),
                       inc=(tt_ == 7))
            wmm = P.c("t")
            for tt_ in range(8):
                rot = tt_ % 2
                qe = "sync" if rot == 0 else "gpsimd"
                LD(bep[5 * rot + 0], x_own[tt_ * 128:(tt_ + 1) * 128, d8 * 512:(d8 + 1) * 512],
                   waits=[("v", P.c("v"))], qe=qe)
                for g in range(NG):
                    LD(bep[5 * rot + 1 + g], sc_attn[g, tt_][:, d8 * 512:(d8 + 1) * 512],
                       waits=([c_end_st] if d8 == 0 and g == 0 else []), qe=qe)
                wld = LMARK(qe)
                VE(lambda e, tt_=tt_, rot=rot: e.tensor_tensor(
                    out=bout[rot], in0=pb[tt_][:], in1=bep[5 * rot + 0], op=OP.add),
                   waits=[wld, ("t", wmm), SMARK()])
                for g in range(NG):
                    VE(lambda e, g=g, rot=rot: e.tensor_tensor(
                        out=bout[rot], in0=bout[rot], in1=bep[5 * rot + 1 + g], op=OP.add),
                       inc=(g == NG - 1))
                wv = P.c("v")
                ST(out[tt_ * 128:(tt_ + 1) * 128, d8 * 512:(d8 + 1) * 512], bout[rot],
                   waits=[("v", wv)])
        P.emit("gpsimd", lambda e: e.nop(), [("st", P.c("st"))], False)

        def run(eng_name, proxy):
            own = P.esem[eng_name]
            self_serial = eng_name in ("vector", "scalar")
            idx = 0
            for waits, fn, isem, q in P.items[eng_name]:
                seen = {}
                if self_serial and idx > 0:
                    seen[own] = idx
                for s, val in waits:
                    if val > 0:
                        seen[s] = max(seen.get(s, 0), val)
                for s, val in seen.items():
                    proxy.wait_ge(sem[s], val)
                inst = fn(proxy)
                if isem is not None:
                    inst.then_inc(sem[isem], q)
                    if isem == own:
                        idx += q

        @block.sync
        def _(sync):
            run("sync", sync)

        @block.tensor
        def _(tensor):
            run("tensor", tensor)

        @block.vector
        def _(vector):
            run("vector", vector)

        @block.scalar
        def _(scalar):
            run("scalar", scalar)

        @block.gpsimd
        def _(gpsimd):
            run("gpsimd", gpsimd)

    return nc


def host_inputs(x, Wq, Wk, Wv, Wo, w1, w2, w3):
    inv = 500000.0 ** (-(np.arange(0, 128, 2, dtype=np.float64) / 128.0))
    pos = np.arange(T, dtype=np.float64)
    fr = np.outer(pos, inv)
    c64, s64 = np.cos(fr), np.sin(fr)
    scl = 1.0 / np.sqrt(np.float64(128.0))

    def hilo(a64):
        hi = a64.astype(np.float32)
        lo = (a64 - hi.astype(np.float64)).astype(np.float32)
        return hi, lo

    ch, cl = hilo(c64)
    sh, sl = hilo(s64)
    rope_cov = np.ascontiguousarray(np.stack([ch, cl, sh, sl]))
    chq, clq = hilo(c64 * scl)
    shq, slq = hilo(s64 * scl)
    rope_q = np.stack([chq, clq, shq, slq])

    Wq_t = np.zeros((NG, 32, 128, 1024), np.float32)
    Wkv_t = np.zeros((NG, 32, 128, 512), np.float32)
    for g in range(NG):
        for ds in range(32):
            Wq_t[g, ds] = Wq[ds * 128:(ds + 1) * 128, g * 1024:(g + 1) * 1024]
            Wkv_t[g, ds, :, 0:256] = Wk[ds * 128:(ds + 1) * 128, g * 256:(g + 1) * 256]
            Wkv_t[g, ds, :, 256:512] = Wv[ds * 128:(ds + 1) * 128, g * 256:(g + 1) * 256]

    ident = np.eye(128, dtype=np.float32)
    ones = np.ones((128, 128), dtype=np.float32)
    maps = []
    for core in range(8):
        b, p1 = core // 4, core % 4
        p2 = 7 - p1
        xb = np.asarray(x[b])
        x_own_ = np.concatenate([xb[p1 * 512:(p1 + 1) * 512], xb[p2 * 512:(p2 + 1) * 512]])
        ro = np.concatenate([rope_q[:, p1 * 512:(p1 + 1) * 512],
                             rope_q[:, p2 * 512:(p2 + 1) * 512]], axis=1)
        m01_ = np.zeros((12, 4, 128, 512), np.float32)
        for jj in range(12):
            qc, kb = (0, jj) if jj < 4 else (1, jj - 4)
            posq = (p1 if qc == 0 else p2) * 512 + np.arange(512)
            for tks in range(4):
                posk = kb * 512 + tks * 128 + np.arange(128)
                m01_[jj, tks] = (posk[:, None] <= posq[None, :]).astype(np.float32)
        maps.append(dict(
            x_own=np.ascontiguousarray(x_own_), x_cov=np.ascontiguousarray(xb),
            Wq=Wq_t, Wkv=Wkv_t, Wo=np.ascontiguousarray(Wo),
            w1=np.ascontiguousarray(w1), w2=np.ascontiguousarray(w2),
            w3=np.ascontiguousarray(w3),
            rope_own=np.ascontiguousarray(ro.astype(np.float32)), rope_cov=rope_cov,
            m01=m01_, ident=ident, ones=ones))
    return maps


def kernel(L0_attn_x, L0_attn_Wq, L0_attn_Wk, L0_attn_Wv, L0_attn_Wo,
           L0_ffn_w1, L0_ffn_w2, L0_ffn_w3):
    x = np.asarray(L0_attn_x, dtype=np.float32)
    maps = host_inputs(x, np.asarray(L0_attn_Wq), np.asarray(L0_attn_Wk),
                       np.asarray(L0_attn_Wv), np.asarray(L0_attn_Wo),
                       np.asarray(L0_ffn_w1), np.asarray(L0_ffn_w2), np.asarray(L0_ffn_w3))
    nc = build_program()
    res = run_bass_kernel_spmd(nc, maps, core_ids=list(range(8)))
    outp = np.empty((B, T, D), dtype=np.float32)
    for core in range(8):
        b, p1 = core // 4, core % 4
        p2 = 7 - p1
        o = res.results[core]["out"]
        outp[b, p1 * 512:(p1 + 1) * 512] = o[0:512]
        outp[b, p2 * 512:(p2 + 1) * 512] = o[512:1024]
    return outp


# version 08
# speedup vs baseline: 1.2048x; 1.0418x over previous
import math
import numpy as np
import contextlib
import concourse.bass as bass
import concourse.mybir as mybir
from concourse.bass_utils import run_bass_kernel_spmd
from concourse.alu_op_type import AluOpType

F32 = mybir.dt.float32
AF = mybir.ActivationFunctionType
AX = mybir.AxisListType
OP = AluOpType

B, T, D, HFF = 2, 4096, 4096, 11008
NG = 4
TOWN = 1024
NHT = HFF // 128
EPS = 1e-5

CODE64 = np.array([-1.0, -0.6961928009986877, -0.5250730514526367, -0.39491748809814453,
                   -0.28444138169288635, -0.18477343022823334, -0.09105003625154495, 0.0,
                   0.07958029955625534, 0.16093020141124725, 0.24611230194568634,
                   0.33791524171829224, 0.44070982933044434, 0.5626170039176941,
                   0.7229568362236023, 1.0])
B64 = (CODE64[:-1] + CODE64[1:]) * 0.5
BHI = B64.astype(np.float32)
BLO = (B64 - BHI.astype(np.float64)).astype(np.float32)
DLT = np.diff(CODE64).astype(np.float32)
C0 = float(np.float32(CODE64[0]))

ENGS = ("sync", "tensor", "vector", "scalar", "gpsimd")


class Plan:
    def __init__(self):
        self.items = {e: [] for e in ENGS}
        self.cnt = {"ld": 0, "st": 0, "t": 0, "v": 0, "a": 0, "sl": 0}
        self.esem = {"sync": "ld", "gpsimd": "st", "tensor": "t", "vector": "v", "scalar": "a"}

    def emit(self, eng, fn, waits=(), inc=False, dma=False):
        q = 16 if (dma or eng in ("sync", "gpsimd")) else 1
        sem = self.esem[eng]
        if dma and eng == "scalar":
            sem = "sl"
            inc = True
        elif eng in ("vector", "scalar"):
            inc = True
        self.items[eng].append((list(waits), fn, sem if inc else None, q))
        if inc:
            self.cnt[sem] += q

    def c(self, s):
        return self.cnt[s]


def build_program():
    nc = bass.Bass("TRN2", target_bir_lowering=False)
    dt = nc.dram_tensor
    x_own = dt("x_own", [TOWN, D], F32, kind="ExternalInput").ap()
    x_cov = dt("x_cov", [T, D], F32, kind="ExternalInput").ap()
    Wq = dt("Wq", [NG, 32, 128, 1024], F32, kind="ExternalInput").ap()
    Wkv = dt("Wkv", [NG, 32, 128, 512], F32, kind="ExternalInput").ap()
    Wo = dt("Wo", [D, D], F32, kind="ExternalInput").ap()
    w1 = dt("w1", [HFF, D], F32, kind="ExternalInput").ap()
    w2 = dt("w2", [HFF, D], F32, kind="ExternalInput").ap()
    w3 = dt("w3", [D, HFF], F32, kind="ExternalInput").ap()
    rope_own = dt("rope_own", [4, TOWN, 64], F32, kind="ExternalInput").ap()
    rope_cov = dt("rope_cov", [4, T, 64], F32, kind="ExternalInput").ap()
    m01 = dt("m01", [12, 4, 128, 512], F32, kind="ExternalInput").ap()
    ident_in = dt("ident", [128, 128], F32, kind="ExternalInput").ap()
    ones_in = dt("ones", [128, 128], F32, kind="ExternalInput").ap()
    out = dt("out", [TOWN, D], F32, kind="ExternalOutput").ap()

    sc_xnT_cov = dt("sc_xnT_cov", [8, 128, 16384], F32).ap()
    sc_xnT_own = dt("sc_xnT_own", [2, 128, 16384], F32).ap()
    sc_dq3T = dt("sc_dq3T", [NHT, 32, 128, 128], F32).ap()
    sc_hT = dt("sc_hT", [NHT, 128, 1024], F32).ap()
    sc_attn = dt("sc_attn", [NG, 8, 128, 4096], F32).ap()

    P = Plan()
    with contextlib.ExitStack() as es:
        sbuf = lambda name, cols: es.enter_context(nc.sbuf_tensor(name, [128, cols], F32))
        AR1 = sbuf("AR1", 24576)
        AR2 = sbuf("AR2", 16384)
        AR3 = sbuf("AR3", 5120)
        SM = sbuf("SM", 192)
        RPO = sbuf("RPO", 256)
        RPC = sbuf("RPC", 256)
        IDT = sbuf("IDT", 128)
        ONE = sbuf("ONE", 128)
        SUMS = sbuf("SUMS", 2048)
        pb = [es.enter_context(nc.psum_tensor("pb%d" % i, [128, 512], F32)) for i in range(8)]
        sem = {k: es.enter_context(nc.semaphore("sem_" + k))
               for k in ("ld", "st", "t", "v", "a", "sl")}
        block = es.enter_context(nc.Block())

        nc.const_aps.register(F32, EPS, SM[:, 8:9])

        def LD(dst, src, waits=(), qe="sync"):
            P.emit(qe, lambda e, d=dst, s=src: e.dma_start(out=d, in_=s), waits, True, dma=True)

        def ST(dst, src, waits=()):
            P.emit("gpsimd", lambda e, d=dst, s=src: e.dma_start(out=d, in_=s), waits, True)

        def LMARK(qe="sync"):
            sm = {"sync": "ld", "gpsimd": "st", "scalar": "sl"}[qe]
            P.emit(qe, lambda e: e.nop(), [(sm, P.c(sm))], False)
            return (sm, P.c(sm))

        def SMARK():
            P.emit("gpsimd", lambda e: e.nop(), [("st", P.c("st"))], False)
            return ("st", P.c("st"))

        TE = lambda fn, waits=(), inc=False: P.emit("tensor", fn, waits, inc)
        VE = lambda fn, waits=(), inc=False: P.emit("vector", fn, waits, inc)
        AE = lambda fn, waits=(), inc=False: P.emit("scalar", fn, waits, inc)

        VE(lambda e: e.memset(SM[:, 8:9], EPS))
        LD(IDT[:], ident_in)
        LD(ONE[:], ones_in)
        w_consts = [LMARK(), ("v", P.c("v"))]

        bx = AR1[:, 0:4096]
        bsq = AR1[:, 4096:8192]
        bxn = AR1[:, 8192:12288]
        basm = AR2[:, 0:16384]
        s_sum, s_sq = SM[:, 0:1], SM[:, 1:2]
        s_m, s_ex2 = SM[:, 2:3], SM[:, 3:4]
        s_mm, s_var = SM[:, 4:5], SM[:, 5:6]
        s_sd, s_r = SM[:, 6:7], SM[:, 7:8]
        s_mr, s_b = SM[:, 10:11], SM[:, 11:12]

        blocks = [("cov", kb) for kb in range(8)] + [("own", h) for h in range(2)]
        xnT_ready = {}
        for bid, (kind, idx) in enumerate(blocks):
            src = x_cov if kind == "cov" else x_own
            for ts_ in range(4):
                row0 = idx * 512 + ts_ * 128
                qe = "sync" if ts_ % 2 == 0 else "gpsimd"
                LD(bx, src[row0:row0 + 128, :],
                   waits=[("v", P.c("v")), ("a", P.c("a"))] + (w_consts if bid == 0 and ts_ == 0 else []),
                   qe=qe)
                wld = LMARK(qe)
                VE(lambda e: e.tensor_reduce(out=s_sum, in_=bx, axis=AX.X, op=OP.add),
                   waits=[wld])
                AE(lambda e: e.activation(bsq, bx, AF.Square),
                   waits=[wld, ("v", P.c("v"))], inc=True)
                VE(lambda e: e.tensor_reduce(out=s_sq, in_=bsq, axis=AX.X, op=OP.add),
                   waits=[("a", P.c("a"))])
                VE(lambda e: e.tensor_scalar_mul(s_m, s_sum, 1.0 / D))
                VE(lambda e: e.tensor_scalar_mul(s_ex2, s_sq, 1.0 / D))
                VE(lambda e: e.tensor_tensor(out=s_mm, in0=s_m, in1=s_m, op=OP.mult))
                VE(lambda e: e.tensor_tensor(out=s_var, in0=s_ex2, in1=s_mm, op=OP.subtract), inc=True)
                AE(lambda e: e.activation(s_sd, s_var, AF.Sqrt, bias=EPS),
                   waits=[("v", P.c("v"))], inc=True)
                VE(lambda e: e.reciprocal(s_r, s_sd), waits=[("a", P.c("a"))])
                VE(lambda e: e.tensor_tensor(out=s_mr, in0=s_m, in1=s_r, op=OP.mult))
                VE(lambda e: e.tensor_scalar_mul(s_b, s_mr, -1.0), inc=True)
                AE(lambda e: e.activation(bxn, bx, AF.Identity, bias=s_b, scale=s_r),
                   waits=[("v", P.c("v"))], inc=True)
                for grp in range(8):
                    for j in range(4):
                        ds = grp * 4 + j
                        TE(lambda e, j=j, ds=ds: e.transpose(
                            pb[6][:, j * 128:(j + 1) * 128], bxn[:, ds * 128:(ds + 1) * 128], IDT[:]),
                           waits=([("a", P.c("a"))] if j == 0 else []), inc=(j == 3))
                    wt = P.c("t")
                    for j in range(4):
                        ds = grp * 4 + j
                        cc = ds * 512 + ts_ * 128
                        AE(lambda e, j=j, cc=cc: e.copy(
                            basm[:, cc:cc + 128], pb[6][:, j * 128:(j + 1) * 128]),
                           waits=[("t", wt)], inc=(j == 3))
            dstb = (sc_xnT_cov if kind == "cov" else sc_xnT_own)[idx]
            ST(dstb, basm, waits=[("a", P.c("a"))])
            xnT_ready[(kind, idx)] = SMARK()
            if bid + 1 < len(blocks):
                AE(lambda e: e.nop(), waits=[xnT_ready[(kind, idx)]], inc=True)

        bwaits = [LMARK(), SMARK()] + [(k, P.c(k)) for k in ("t", "v", "a")]
        P.emit("sync", lambda e: e.nop(), [w for w in bwaits if w[0] != "lg"], False)
        P.emit("gpsimd", lambda e: e.nop(), [w for w in bwaits if w[0] != "sg"], False)
        P.emit("tensor", lambda e: e.nop(), [w for w in bwaits if w[0] != "t"], True)
        P.emit("vector", lambda e: e.nop(), [w for w in bwaits if w[0] != "v"], True)
        P.emit("scalar", lambda e: e.nop(), [w for w in bwaits if w[0] != "a"], True)

        bprobs = AR1[:, 8192:10240]
        m1b = bprobs[:, 0:512]
        m2b = bprobs[:, 512:1024]
        o1b = bprobs[:, 1024:1536]
        o2b = bprobs[:, 1536:2048]
        braw = AR1[:, 10240:12288]
        bvv = AR1[:, 12288:14336]
        btmp = AR1[:, 14336:16384]
        bacc = [AR1[:, 16384:18432], AR1[:, 18432:20480]]
        bdqa = AR1[:, 20480:24576]
        s_am = SM[:, 64:128]
        s_rc = SM[:, 128:192]
        qTb = AR1[:, 0:8192]
        kTb = AR2[:, 0:8192]
        vTb = AR2[:, 8192:16384]
        bwo = AR2[:, 0:8192]
        batn = AR2[:, 8192:12288]
        xa = [AR3[:, 0:512], AR3[:, 512:1024]]
        xb = [AR3[:, 1024:1536], AR3[:, 1536:2048]]
        wpc = [AR3[:, 2048:3072], AR3[:, 3072:4096]]
        bqro = AR3[:, 4096:4608]
        bmsk = AR3[:, 4608:5120]
        rop = [RPO[:, i * 64:(i + 1) * 64] for i in range(4)]
        rcv = [RPC[:, i * 64:(i + 1) * 64] for i in range(4)]
        attn_mark = [None]

        spill = {}
        dq_state = {"wt_prev": 0}

        dqst = {"cons": [0, 0], "slot": 0, "stg": 0}

        def dequant_job(tag, idx, dst_asm=None, asm_guard=(), stg=None):
            if tag == "w1":
                src = w1
                chunks = [(0, 2048), (2048, 2048)]
            elif tag == "w2":
                src = w2
                chunks = [(0, 2048), (2048, 2048)]
            else:
                src = w3
                chunks = [(k * 2048, 2048) for k in range(5)] + [(10240, 768)]
            first_chunk = True
            for (c0, cw) in chunks:
                nb = cw // 64
                slot = dqst["slot"]
                dqst["slot"] = 1 - slot
                brw = braw[slot]
                qe = "scalar" if slot == 0 else "gpsimd"
                LD(brw[:, 0:cw], src[idx * 128:(idx + 1) * 128, c0:c0 + cw],
                   waits=[("v", dqst["cons"][slot])], qe=qe)
                wld = LMARK(qe)
                rawv = brw[:, 0:cw].rearrange("p (b j) -> p b j", j=64)
                VE(lambda e, rawv=rawv, nb=nb: e.tensor_reduce(
                    out=s_am[:, 0:nb], in_=rawv, axis=AX.X, op=OP.max, apply_absolute_value=True),
                   waits=[wld])
                VE(lambda e, nb=nb: e.reciprocal(s_rc[:, 0:nb], s_am[:, 0:nb]))
                rbc = s_rc[:, 0:nb].unsqueeze(2).broadcast_to((128, nb, 64))
                VE(lambda e, rawv=rawv, rbc=rbc, cw=cw: e.tensor_tensor(
                    out=bvv[:, 0:cw].rearrange("p (b j) -> p b j", j=64), in0=rawv, in1=rbc,
                    op=OP.mult), inc=True)
                dqst["cons"][slot] = P.c("v")
                VE(lambda e, cw=cw: e.tensor_scalar(out=btmp[:, 0:cw], in0=bvv[:, 0:cw],
                                                    scalar1=float(BHI[0]), scalar2=float(BLO[0]),
                                                    op0=OP.subtract, op1=OP.is_gt))
                VE(lambda e, cw=cw: e.tensor_scalar_mul(bacc[0][:, 0:cw], btmp[:, 0:cw],
                                                        float(DLT[0])),
                   waits=[("t", dq_state["wt_prev"])])
                a = 0
                for i in range(1, 15):
                    VE(lambda e, cw=cw, i=i: e.tensor_scalar(
                        out=btmp[:, 0:cw], in0=bvv[:, 0:cw], scalar1=float(BHI[i]),
                        scalar2=float(BLO[i]), op0=OP.subtract, op1=OP.is_gt))
                    VE(lambda e, cw=cw, i=i, a=a: e.scalar_tensor_tensor(
                        out=bacc[1 - a][:, 0:cw], in0=btmp[:, 0:cw], scalar=float(DLT[i]),
                        in1=bacc[a][:, 0:cw], op0=OP.mult, op1=OP.add))
                    a = 1 - a
                abc = s_am[:, 0:nb].unsqueeze(2).broadcast_to((128, nb, 64))
                VE(lambda e, cw=cw, a=a, abc=abc: e.scalar_tensor_tensor(
                    out=bacc[1 - a][:, 0:cw].rearrange("p (b j) -> p b j", j=64),
                    in0=bacc[a][:, 0:cw].rearrange("p (b j) -> p b j", j=64),
                    scalar=C0, in1=abc, op0=OP.add, op1=OP.mult), inc=True)
                wdq = P.c("v")
                dqbuf = bacc[1 - a]
                npc = cw // 128
                for p0 in range(0, npc, 4):
                    n4 = min(4, npc - p0)
                    for j in range(n4):
                        TE(lambda e, j=j, dqbuf=dqbuf, p0=p0: e.transpose(
                            pb[7][:, j * 128:(j + 1) * 128],
                            dqbuf[:, (p0 + j) * 128:(p0 + j + 1) * 128], IDT[:]),
                           waits=([("v", wdq), ("a", P.c("a"))] if j == 0 else []),
                           inc=(j == n4 - 1))
                    wt = P.c("t")
                    ev_waits = [("t", wt)]
                    if tag != "w3":
                        if first_chunk and p0 == 0:
                            ev_waits.extend(asm_guard)
                        for j in range(n4):
                            gp = c0 // 128 + p0 + j
                            AE(lambda e, j=j, gp=gp, dst_asm=dst_asm: e.copy(
                                dst_asm[:, gp * 128:(gp + 1) * 128], pb[7][:, j * 128:(j + 1) * 128]),
                               waits=(ev_waits if j == 0 else []), inc=(j == n4 - 1))
                    else:
                        ev_waits.append(SMARK())
                        for j in range(n4):
                            AE(lambda e, j=j, stg=stg: e.copy(
                                stg[:, j * 128:(j + 1) * 128], pb[7][:, j * 128:(j + 1) * 128]),
                               waits=(ev_waits if j == 0 else []), inc=(j == n4 - 1))
                        wa = P.c("a")
                        for j in range(n4):
                            ht = c0 // 128 + p0 + j
                            ST(sc_dq3T[ht, idx], stg[:, j * 128:(j + 1) * 128], waits=[("a", wa)])
                dq_state["wt_prev"] = P.c("t")
                first_chunk = False
            if tag == "w3":
                spill[(tag, idx)] = SMARK()



        def rope_apply(mk, nh, tabs, dst, dstoff, waits0):
            ch, cl, sh, sl = tabs
            bc = lambda t: t.unsqueeze(1).broadcast_to((128, nh, 64))
            x1 = mk(0, 64)
            x2 = mk(64, 128)
            dview = dst[:, dstoff:dstoff + nh * 128].rearrange("p (h j) -> p h j", j=128)
            dst1 = dview[:, :, 0:64]
            dst2 = dview[:, :, 64:128]
            mm1 = m1b[:, 0:nh * 64].rearrange("p (h j) -> p h j", j=64)
            mm2 = m2b[:, 0:nh * 64].rearrange("p (h j) -> p h j", j=64)
            oo1 = o1b[:, 0:nh * 64].rearrange("p (h j) -> p h j", j=64)
            oo2 = o2b[:, 0:nh * 64].rearrange("p (h j) -> p h j", j=64)
            VE(lambda e: e.tensor_tensor(out=mm1, in0=x1, in1=bc(ch), op=OP.mult), waits=waits0)
            VE(lambda e: e.tensor_tensor(out=mm2, in0=x2, in1=bc(sh), op=OP.mult))
            VE(lambda e: e.tensor_tensor(out=oo1, in0=mm1, in1=mm2, op=OP.subtract))
            VE(lambda e: e.tensor_tensor(out=mm1, in0=x1, in1=bc(cl), op=OP.mult))
            VE(lambda e: e.tensor_tensor(out=mm2, in0=x2, in1=bc(sl), op=OP.mult))
            VE(lambda e: e.tensor_tensor(out=oo2, in0=mm1, in1=mm2, op=OP.subtract))
            VE(lambda e: e.tensor_tensor(out=dst1, in0=oo1, in1=oo2, op=OP.add))
            VE(lambda e: e.tensor_tensor(out=mm1, in0=x1, in1=bc(sh), op=OP.mult))
            VE(lambda e: e.tensor_tensor(out=mm2, in0=x2, in1=bc(ch), op=OP.mult))
            VE(lambda e: e.tensor_tensor(out=oo1, in0=mm1, in1=mm2, op=OP.add))
            VE(lambda e: e.tensor_tensor(out=mm1, in0=x1, in1=bc(sl), op=OP.mult))
            VE(lambda e: e.tensor_tensor(out=mm2, in0=x2, in1=bc(cl), op=OP.mult))
            VE(lambda e: e.tensor_tensor(out=oo2, in0=mm1, in1=mm2, op=OP.add))
            VE(lambda e: e.tensor_tensor(out=dst2, in0=oo1, in1=oo2, op=OP.add), inc=True)

        for g in range(NG):
            for pr in range(4):
                for ds in range(32):
                    rot = ds % 2
                    qe = "sync" if rot == 0 else "gpsimd"
                    LD(wpc[rot][:, 0:1024], Wq[g, ds], waits=[("t", P.c("t"))], qe=qe)
                    for ti in range(2):
                        tile = pr * 2 + ti
                        half, tsub = tile // 4, tile % 4
                        cc = ds * 512 + tsub * 128
                        LD((xa if ti == 0 else xb)[rot][:, 0:128],
                           sc_xnT_own[half][:, cc:cc + 128],
                           waits=[xnT_ready[("own", half)], ("t", P.c("t"))], qe=qe)
                    wld = LMARK(qe)
                    for ti in range(2):
                        xsrc = (xa if ti == 0 else xb)[rot][:, 0:128]
                        for nbk in range(2):
                            TE(lambda e, ti=ti, nbk=nbk, xsrc=xsrc, rot=rot, ds=ds: e.matmul(
                                pb[2 * ti + nbk][:], xsrc,
                                wpc[rot][:, nbk * 512:(nbk + 1) * 512],
                                start=(ds == 0), stop=(ds == 31)),
                               waits=([wld] +
                                      ([("v", P.c("v")), ("a", P.c("a"))] if ds == 0 else [])),
                               inc=(ti == 1 and nbk == 1))
                wmm = P.c("t")
                for ti in range(2):
                    tile = pr * 2 + ti
                    qe = "sync" if ti % 2 == 0 else "gpsimd"
                    for i in range(4):
                        LD(rop[i], rope_own[i, tile * 128:(tile + 1) * 128, :],
                           waits=[("v", P.c("v"))], qe=qe)
                    wld2 = LMARK(qe)
                    for bk in range(2):
                        bank = pb[2 * ti + bk]
                        mk = lambda lo, hi, bank=bank: bank[:, :].rearrange(
                            "p (h j) -> p h j", j=128)[:, :, lo:hi]
                        rope_apply(mk, 4, rop, bqro, 0,
                                   [wld2, ("t", wmm), ("a", P.c("a"))])
                        wv = P.c("v")
                        for j in range(4):
                            TE(lambda e, j=j: e.transpose(
                                pb[4][:, j * 128:(j + 1) * 128], bqro[:, j * 128:(j + 1) * 128],
                                IDT[:]),
                               waits=([("v", wv), ("a", P.c("a"))] if j == 0 else []),
                               inc=(j == 3))
                        wt = P.c("t")
                        for j in range(4):
                            h = bk * 4 + j
                            c1 = h * 1024 + tile * 128
                            AE(lambda e, j=j, c1=c1: e.copy(
                                qTb[:, c1:c1 + 128], pb[4][:, j * 128:(j + 1) * 128]),
                               waits=[("t", wt)], inc=(j == 3))
            for kb in range(8):
                for ds in range(32):
                    rot = ds % 2
                    qe = "sync" if rot == 0 else "gpsimd"
                    LD(wpc[rot][:, 0:512], Wkv[g, ds], waits=[("t", P.c("t"))], qe=qe)
                    LD(xa[rot][:, 0:512], sc_xnT_cov[kb][:, ds * 512:ds * 512 + 512],
                       waits=[xnT_ready[("cov", kb)], ("t", P.c("t"))], qe=qe)
                    wld = LMARK(qe)
                    for tks in range(4):
                        TE(lambda e, tks=tks, rot=rot, ds=ds: e.matmul(
                            pb[tks][:], xa[rot][:, tks * 128:(tks + 1) * 128],
                            wpc[rot][:, 0:512], start=(ds == 0), stop=(ds == 31)),
                           waits=([wld] +
                                  ([("v", P.c("v")), ("a", P.c("a"))] if ds == 0 else [])),
                           inc=(tks == 3))
                wmm = P.c("t")
                for tks in range(4):
                    qe = "sync" if tks % 2 == 0 else "gpsimd"
                    for i in range(4):
                        LD(rcv[i], rope_cov[i, kb * 512 + tks * 128: kb * 512 + (tks + 1) * 128, :],
                           waits=[("v", P.c("v"))], qe=qe)
                    wld2 = LMARK(qe)
                    bank = pb[tks]
                    mk = lambda lo, hi, bank=bank: bank[:, 0:256].rearrange(
                        "p (h j) -> p h j", j=128)[:, :, lo:hi]
                    rope_apply(mk, 2, rcv, bqro, 0, [wld2, ("t", wmm), ("a", P.c("a"))])
                    wv = P.c("v")
                    for j in range(2):
                        TE(lambda e, j=j: e.transpose(
                            pb[4][:, j * 128:(j + 1) * 128], bqro[:, j * 128:(j + 1) * 128], IDT[:]),
                           waits=([("v", wv), ("a", P.c("a"))] if j == 0 else []), inc=(j == 1))
                    wt = P.c("t")
                    for j in range(2):
                        c1 = j * 4096 + kb * 512 + tks * 128
                        AE(lambda e, j=j, c1=c1: e.copy(
                            kTb[:, c1:c1 + 128], pb[4][:, j * 128:(j + 1) * 128]),
                           waits=[("t", wt)] +
                                 ([attn_mark[0]] if (kb == 0 and tks == 0 and j == 0
                                                     and attn_mark[0]) else []),
                           inc=(j == 1))
                    for j in range(2):
                        c1 = ((kb * 4 + tks) * 2 + j) * 128
                        AE(lambda e, j=j, c1=c1, tks=tks: e.copy(
                            vTb[:, c1:c1 + 128], pb[tks][:, 256 + j * 128:256 + (j + 1) * 128]),
                           inc=(j == 1))
            bank_exp = [0, 0]
            last_mult_v = 0
            last_avsums_t = 0
            bpm = AR1[:, 10240:12288]
            mskblk = [AR1[:, 12288:14336], AR1[:, 14336:16384]]
            msk_v = [0, 0]
            blkctr = [0]
            for h in range(8):
                kvh = h // 4
                hfirst = [("a", P.c("a")), ("v", P.c("v"))]
                for qc in range(2):
                    nkb = 4 if qc == 0 else 8
                    first_blk = True
                    for kb in range(nkb):
                        blkid = kb if qc == 0 else 4 + kb
                        last_blk = (kb == nkb - 1)
                        mrot = blkctr[0] % 2
                        blkctr[0] += 1
                        qe = "sync" if mrot == 0 else "gpsimd"
                        for tks in range(4):
                            LD(mskblk[mrot][:, tks * 512:(tks + 1) * 512], m01[blkid, tks],
                               waits=([("v", msk_v[mrot])] if tks == 0 else []), qe=qe)
                        wmk = LMARK(qe)
                        exw = [0, 0]
                        for tks in range(4):
                            sbank = pb[tks % 2]
                            kcol = kvh * 4096 + kb * 512 + tks * 128
                            qcol = h * 1024 + qc * 512
                            TE(lambda e, sbank=sbank, kcol=kcol, qcol=qcol: e.matmul(
                                sbank[:], kTb[:, kcol:kcol + 128], qTb[:, qcol:qcol + 512],
                                start=True, stop=True),
                               waits=[("a", exw[tks % 2] or bank_exp[tks % 2])] +
                                     (hfirst if (first_blk and kb == 0 and tks == 0 and qc == 0) else []),
                               inc=True)
                            wsc = P.c("t")
                            AE(lambda e, sbank=sbank, tks=tks: e.activation(
                                bprobs[:, tks * 512:(tks + 1) * 512], sbank[:], AF.Exp),
                               waits=[("t", wsc), ("v", last_mult_v)], inc=True)
                            exw[tks % 2] = P.c("a")
                        bank_exp = exw[:]
                        VE(lambda e, mrot=mrot: e.tensor_tensor(
                            out=bpm, in0=bprobs[:, 0:2048], in1=mskblk[mrot], op=OP.mult),
                           waits=[("a", P.c("a")), wmk, ("t", last_avsums_t)], inc=True)
                        wpm = P.c("v")
                        last_mult_v = wpm
                        msk_v[mrot] = wpm
                        for tks in range(4):
                            first = first_blk and tks == 0
                            last = last_blk and tks == 3
                            vcol = ((kb * 4 + tks) * 2 + kvh) * 128
                            TE(lambda e, vcol=vcol, qc=qc, first=first, last=last, tks=tks: e.matmul(
                                pb[2 + qc][:], vTb[:, vcol:vcol + 128],
                                bpm[:, tks * 512:(tks + 1) * 512],
                                start=first, stop=last),
                               waits=([("v", wpm)] if tks == 0 else []), inc=True)
                            TE(lambda e, qc=qc, first=first, last=last, tks=tks: e.matmul(
                                pb[4 + qc][0:1, :], ONE[:, 0:1],
                                bpm[:, tks * 512:(tks + 1) * 512],
                                start=first, stop=last),
                               inc=True)
                        last_avsums_t = P.c("t")
                        first_blk = False
                wavd = P.c("t")
                AE(lambda e: e.copy(SUMS[0:1, 0:512], pb[4][0:1, :]), waits=[("t", wavd)])
                AE(lambda e: e.copy(SUMS[0:1, 512:1024], pb[5][0:1, :]), inc=True)
                VE(lambda e: e.reciprocal(SUMS[0:1, 1024:2048], SUMS[0:1, 0:1024]),
                   waits=[("a", P.c("a"))], inc=True)
                wrs = P.c("v")
                for qc in range(2):
                    TE(lambda e, qc=qc: e.matmul(
                        pb[qc][:], ONE[0:1, 0:128], SUMS[0:1, 1024 + qc * 512:1024 + (qc + 1) * 512],
                        start=True, stop=True),
                       waits=([("v", wrs), ("a", P.c("a"))] if qc == 0 else []), inc=(qc == 1))
                wbc = P.c("t")
                for qc in range(2):
                    AE(lambda e, qc=qc: e.copy(bprobs[:, qc * 512:(qc + 1) * 512], pb[qc][:]),
                       waits=([("t", wbc), ("v", P.c("v"))] if qc == 0 else []), inc=(qc == 1))
                wev = P.c("a")
                for qc in range(2):
                    c1 = h * 1024 + qc * 512
                    VE(lambda e, qc=qc, c1=c1: e.tensor_tensor(
                        out=qTb[:, c1:c1 + 512], in0=pb[2 + qc][:],
                        in1=bprobs[:, qc * 512:(qc + 1) * 512], op=OP.mult),
                       waits=[("a", wev)], inc=(qc == 1))
                last_mult_v = P.c("v")
            for tt_ in range(8):
                for rnd, (nb_, coff) in enumerate([(6, 0), (2, 3072)]):
                    for s in range(8):
                        rot = s % 2
                        qe = "sync" if rot == 0 else "gpsimd"
                        LD(bwo[:, rot * 4096: rot * 4096 + nb_ * 512],
                           Wo[g * 1024 + s * 128: g * 1024 + s * 128 + 128, coff:coff + nb_ * 512],
                           waits=[("t", P.c("t"))] +
                                 ([("v", P.c("v"))] if tt_ == 0 and rnd == 0 and s == 0 else []),
                           qe=qe)
                        wld = LMARK(qe)
                        for c in range(nb_):
                            lcol = (g * 0 + s) * 1024 + tt_ * 128
                            TE(lambda e, c=c, rot=rot, lcol=lcol, s=s: e.matmul(
                                pb[c][:], qTb[:, lcol:lcol + 128],
                                bwo[:, rot * 4096 + c * 512: rot * 4096 + (c + 1) * 512],
                                start=(s == 0), stop=(s == 7)),
                               waits=([wld] +
                                      ([("a", P.c("a")), ("v", P.c("v"))] if s == 0 else [])),
                               inc=(c == nb_ - 1))
                    wmm = P.c("t")
                    for c in range(nb_):
                        AE(lambda e, c=c, coff=coff: e.copy(
                            batn[:, coff + c * 512: coff + (c + 1) * 512], pb[c][:]),
                           waits=([("t", wmm)] + ([SMARK()] if c == 0 else [])),
                           inc=(c == nb_ - 1))
                ST(sc_attn[g, tt_], batn, waits=[("a", P.c("a"))])
            attn_mark[0] = SMARK()
        c_end_st = SMARK()

        braw = [AR1[:, 8192:10240], AR1[:, 10240:12288]]
        bvv = AR1[:, 12288:14336]
        btmp = AR1[:, 14336:16384]
        bacc = [AR1[:, 16384:18432], AR1[:, 18432:20480]]
        bdqa1r = [AR1[:, 0:4096], AR1[:, 4096:8192]]
        bdqa2r = [AR2[:, 8192:12288], AR2[:, 12288:16384]]
        bhD = AR1[:, 20480:21504]
        bsilD = AR1[:, 21504:22016]
        w3stg = AR1[:, 22016:22528]
        xnq = [AR2[:, 0:4096], AR2[:, 4096:8192]]
        hT_ready = {}
        d_boundary = [LMARK(), c_end_st, ("a", P.c("a")), ("v", P.c("v")), ("t", P.c("t"))]
        asm_t = [0, 0]
        asm_ready = [0, 0]
        xnh_t = [0, 0]
        w3q = list(range(32))

        def emit_dq_ht(ht):
            slot = ht % 2
            guard = [("t", asm_t[slot])] + (d_boundary if ht == 0 else [])
            dequant_job("w1", ht, dst_asm=bdqa1r[slot], asm_guard=guard)
            dequant_job("w2", ht, dst_asm=bdqa2r[slot], asm_guard=[("t", asm_t[slot])])
            asm_ready[slot] = P.c("a")

        emit_dq_ht(0)
        for ht in range(NHT):
            slot = ht % 2
            if ht + 1 < NHT:
                emit_dq_ht(ht + 1)
            if ht % 3 == 1 and w3q:
                dequant_job("w3", w3q.pop(0), stg=w3stg)
            wasm = asm_ready[slot]
            for half in range(2):
                for qtr in range(4):
                    rot = qtr % 2
                    qe = "sync"
                    LD(xnq[rot], sc_xnT_own[half][:, qtr * 4096:(qtr + 1) * 4096],
                       waits=[("t", xnh_t[rot])] +
                             (d_boundary if ht == 0 and half == 0 and qtr == 0 else []),
                       qe=qe)
                    wldq = LMARK(qe)
                    for dsl in range(8):
                        ds = qtr * 8 + dsl
                        TE(lambda e, ds=ds, dsl=dsl, rot=rot, half=half, slot=slot: e.matmul(
                            pb[half][:], bdqa1r[slot][:, ds * 128:(ds + 1) * 128],
                            xnq[rot][:, dsl * 512:(dsl + 1) * 512],
                            start=(ds == 0), stop=(ds == 31)),
                           waits=([wldq, ("a", wasm)] +
                                  ([("a", P.c("a")), ("v", P.c("v"))] if ds == 0 else [])))
                        TE(lambda e, ds=ds, dsl=dsl, rot=rot, half=half, slot=slot: e.matmul(
                            pb[2 + half][:], bdqa2r[slot][:, ds * 128:(ds + 1) * 128],
                            xnq[rot][:, dsl * 512:(dsl + 1) * 512],
                            start=(ds == 0), stop=(ds == 31)),
                           inc=True)
                    xnh_t[rot] = P.c("t")
            asm_t[slot] = P.c("t")
            for half in range(2):
                wmmD = P.c("t")
                AE(lambda e, half=half: e.activation(bsilD[:], pb[half][:], AF.Silu),
                   waits=[("t", wmmD), ("v", P.c("v"))], inc=True)
                VE(lambda e, half=half: e.tensor_tensor(
                    out=bhD[:, half * 512:(half + 1) * 512], in0=bsilD[:], in1=pb[2 + half][:],
                    op=OP.mult),
                   waits=[("a", P.c("a"))] +
                         ([hT_ready[ht - 1]] if (half == 0 and ht > 0) else []), inc=True)
            ST(sc_hT[ht], bhD, waits=[("v", P.c("v"))])
            hT_ready[ht] = SMARK()
        while w3q:
            dequant_job("w3", w3q.pop(0), stg=w3stg)

        bhT = [AR1[:, 0:1024], AR1[:, 1024:2048]]
        bd3 = [AR1[:, 2048:2560], AR1[:, 2560:3072]]
        bep = [AR1[:, 3072 + k * 512: 3072 + (k + 1) * 512] for k in range(10)]
        bout = [AR1[:, 8192:8704], AR1[:, 8704:9216]]
        e_boundary = [LMARK(), ("a", P.c("a")), ("v", P.c("v"))]
        for d8 in range(8):
            for ht in range(NHT):
                rot = ht % 2
                qe = "sync" if rot == 0 else "gpsimd"
                LD(bhT[rot], sc_hT[ht], waits=[hT_ready[ht], ("t", P.c("t"))] +
                   (e_boundary if d8 == 0 and ht == 0 else []), qe=qe)
                for j in range(4):
                    LD(bd3[rot][:, j * 128:(j + 1) * 128], sc_dq3T[ht, d8 * 4 + j],
                       waits=([spill[("w3", d8 * 4 + 3)], ("t", P.c("t"))] if j == 0 else []), qe=qe)
                wld = LMARK(qe)
                for tt_ in range(8):
                    TE(lambda e, tt_=tt_, rot=rot, ht=ht: e.matmul(
                        pb[tt_][:], bhT[rot][:, tt_ * 128:(tt_ + 1) * 128], bd3[rot][:, 0:512],
                        start=(ht == 0), stop=(ht == NHT - 1)),
                       waits=([wld] +
                              ([("v", P.c("v")), ("a", P.c("a"))] if ht == 0 else [])),
                       inc=(tt_ == 7))
            wmm = P.c("t")
            for tt_ in range(8):
                rot = tt_ % 2
                qe = "sync" if rot == 0 else "gpsimd"
                LD(bep[5 * rot + 0], x_own[tt_ * 128:(tt_ + 1) * 128, d8 * 512:(d8 + 1) * 512],
                   waits=[("v", P.c("v"))], qe=qe)
                for g in range(NG):
                    LD(bep[5 * rot + 1 + g], sc_attn[g, tt_][:, d8 * 512:(d8 + 1) * 512],
                       waits=([c_end_st] if d8 == 0 and g == 0 else []), qe=qe)
                wld = LMARK(qe)
                VE(lambda e, tt_=tt_, rot=rot: e.tensor_tensor(
                    out=bout[rot], in0=pb[tt_][:], in1=bep[5 * rot + 0], op=OP.add),
                   waits=[wld, ("t", wmm), SMARK()])
                for g in range(NG):
                    VE(lambda e, g=g, rot=rot: e.tensor_tensor(
                        out=bout[rot], in0=bout[rot], in1=bep[5 * rot + 1 + g], op=OP.add),
                       inc=(g == NG - 1))
                wv = P.c("v")
                ST(out[tt_ * 128:(tt_ + 1) * 128, d8 * 512:(d8 + 1) * 512], bout[rot],
                   waits=[("v", wv)])
        P.emit("gpsimd", lambda e: e.nop(), [("st", P.c("st"))], False)

        def run(eng_name, proxy):
            own = P.esem[eng_name]
            self_serial = eng_name in ("vector", "scalar")
            idx = 0
            for waits, fn, isem, q in P.items[eng_name]:
                seen = {}
                if self_serial and idx > 0:
                    seen[own] = idx
                for s, val in waits:
                    if val > 0:
                        seen[s] = max(seen.get(s, 0), val)
                for s, val in seen.items():
                    proxy.wait_ge(sem[s], val)
                inst = fn(proxy)
                if isem is not None:
                    inst.then_inc(sem[isem], q)
                    if isem == own:
                        idx += q

        @block.sync
        def _(sync):
            run("sync", sync)

        @block.tensor
        def _(tensor):
            run("tensor", tensor)

        @block.vector
        def _(vector):
            run("vector", vector)

        @block.scalar
        def _(scalar):
            run("scalar", scalar)

        @block.gpsimd
        def _(gpsimd):
            run("gpsimd", gpsimd)

    return nc


def host_inputs(x, Wq, Wk, Wv, Wo, w1, w2, w3):
    inv = 500000.0 ** (-(np.arange(0, 128, 2, dtype=np.float64) / 128.0))
    pos = np.arange(T, dtype=np.float64)
    fr = np.outer(pos, inv)
    c64, s64 = np.cos(fr), np.sin(fr)
    scl = 1.0 / np.sqrt(np.float64(128.0))

    def hilo(a64):
        hi = a64.astype(np.float32)
        lo = (a64 - hi.astype(np.float64)).astype(np.float32)
        return hi, lo

    ch, cl = hilo(c64)
    sh, sl = hilo(s64)
    rope_cov = np.ascontiguousarray(np.stack([ch, cl, sh, sl]))
    chq, clq = hilo(c64 * scl)
    shq, slq = hilo(s64 * scl)
    rope_q = np.stack([chq, clq, shq, slq])

    Wq_t = np.zeros((NG, 32, 128, 1024), np.float32)
    Wkv_t = np.zeros((NG, 32, 128, 512), np.float32)
    for g in range(NG):
        for ds in range(32):
            Wq_t[g, ds] = Wq[ds * 128:(ds + 1) * 128, g * 1024:(g + 1) * 1024]
            Wkv_t[g, ds, :, 0:256] = Wk[ds * 128:(ds + 1) * 128, g * 256:(g + 1) * 256]
            Wkv_t[g, ds, :, 256:512] = Wv[ds * 128:(ds + 1) * 128, g * 256:(g + 1) * 256]

    ident = np.eye(128, dtype=np.float32)
    ones = np.ones((128, 128), dtype=np.float32)
    maps = []
    for core in range(8):
        b, p1 = core // 4, core % 4
        p2 = 7 - p1
        xb = np.asarray(x[b])
        x_own_ = np.concatenate([xb[p1 * 512:(p1 + 1) * 512], xb[p2 * 512:(p2 + 1) * 512]])
        ro = np.concatenate([rope_q[:, p1 * 512:(p1 + 1) * 512],
                             rope_q[:, p2 * 512:(p2 + 1) * 512]], axis=1)
        m01_ = np.zeros((12, 4, 128, 512), np.float32)
        for jj in range(12):
            qc, kb = (0, jj) if jj < 4 else (1, jj - 4)
            posq = (p1 if qc == 0 else p2) * 512 + np.arange(512)
            for tks in range(4):
                posk = kb * 512 + tks * 128 + np.arange(128)
                m01_[jj, tks] = (posk[:, None] <= posq[None, :]).astype(np.float32)
        maps.append(dict(
            x_own=np.ascontiguousarray(x_own_), x_cov=np.ascontiguousarray(xb),
            Wq=Wq_t, Wkv=Wkv_t, Wo=np.ascontiguousarray(Wo),
            w1=np.ascontiguousarray(w1), w2=np.ascontiguousarray(w2),
            w3=np.ascontiguousarray(w3),
            rope_own=np.ascontiguousarray(ro.astype(np.float32)), rope_cov=rope_cov,
            m01=m01_, ident=ident, ones=ones))
    return maps


def kernel(L0_attn_x, L0_attn_Wq, L0_attn_Wk, L0_attn_Wv, L0_attn_Wo,
           L0_ffn_w1, L0_ffn_w2, L0_ffn_w3):
    x = np.asarray(L0_attn_x, dtype=np.float32)
    maps = host_inputs(x, np.asarray(L0_attn_Wq), np.asarray(L0_attn_Wk),
                       np.asarray(L0_attn_Wv), np.asarray(L0_attn_Wo),
                       np.asarray(L0_ffn_w1), np.asarray(L0_ffn_w2), np.asarray(L0_ffn_w3))
    nc = build_program()
    res = run_bass_kernel_spmd(nc, maps, core_ids=list(range(8)))
    outp = np.empty((B, T, D), dtype=np.float32)
    for core in range(8):
        b, p1 = core // 4, core % 4
        p2 = 7 - p1
        o = res.results[core]["out"]
        outp[b, p1 * 512:(p1 + 1) * 512] = o[0:512]
        outp[b, p2 * 512:(p2 + 1) * 512] = o[512:1024]
    return outp


# version 09 (final)
# speedup vs baseline: 1.2134x; 1.0071x over previous
import math
import numpy as np
import contextlib
import concourse.bass as bass
import concourse.mybir as mybir
from concourse.bass_utils import run_bass_kernel_spmd
from concourse.alu_op_type import AluOpType

F32 = mybir.dt.float32
AF = mybir.ActivationFunctionType
AX = mybir.AxisListType
OP = AluOpType

B, T, D, HFF = 2, 4096, 4096, 11008
NG = 4
TOWN = 1024
NHT = HFF // 128
EPS = 1e-5

CODE64 = np.array([-1.0, -0.6961928009986877, -0.5250730514526367, -0.39491748809814453,
                   -0.28444138169288635, -0.18477343022823334, -0.09105003625154495, 0.0,
                   0.07958029955625534, 0.16093020141124725, 0.24611230194568634,
                   0.33791524171829224, 0.44070982933044434, 0.5626170039176941,
                   0.7229568362236023, 1.0])
B64 = (CODE64[:-1] + CODE64[1:]) * 0.5
BHI = B64.astype(np.float32)
BLO = (B64 - BHI.astype(np.float64)).astype(np.float32)
DLT = np.diff(CODE64).astype(np.float32)
C0 = float(np.float32(CODE64[0]))

ENGS = ("sync", "tensor", "vector", "scalar", "gpsimd")


class Plan:
    def __init__(self):
        self.items = {e: [] for e in ENGS}
        self.cnt = {"ld": 0, "st": 0, "t": 0, "v": 0, "a": 0, "sl": 0}
        self.esem = {"sync": "ld", "gpsimd": "st", "tensor": "t", "vector": "v", "scalar": "a"}

    def emit(self, eng, fn, waits=(), inc=False, dma=False):
        q = 16 if (dma or eng in ("sync", "gpsimd")) else 1
        sem = self.esem[eng]
        if dma and eng == "scalar":
            sem = "sl"
            inc = True
        elif eng in ("vector", "scalar"):
            inc = True
        self.items[eng].append((list(waits), fn, sem if inc else None, q))
        if inc:
            self.cnt[sem] += q

    def c(self, s):
        return self.cnt[s]


def build_program():
    nc = bass.Bass("TRN2", target_bir_lowering=False)
    dt = nc.dram_tensor
    x_own = dt("x_own", [TOWN, D], F32, kind="ExternalInput").ap()
    x_cov = dt("x_cov", [T, D], F32, kind="ExternalInput").ap()
    Wq = dt("Wq", [NG, 32, 128, 1024], F32, kind="ExternalInput").ap()
    Wkv = dt("Wkv", [NG, 32, 128, 512], F32, kind="ExternalInput").ap()
    Wo = dt("Wo", [D, D], F32, kind="ExternalInput").ap()
    w1 = dt("w1", [HFF, D], F32, kind="ExternalInput").ap()
    w2 = dt("w2", [HFF, D], F32, kind="ExternalInput").ap()
    w3 = dt("w3", [D, HFF], F32, kind="ExternalInput").ap()
    rope_own = dt("rope_own", [4, TOWN, 64], F32, kind="ExternalInput").ap()
    rope_cov = dt("rope_cov", [4, T, 64], F32, kind="ExternalInput").ap()
    m01 = dt("m01", [12, 4, 128, 512], F32, kind="ExternalInput").ap()
    ident_in = dt("ident", [128, 128], F32, kind="ExternalInput").ap()
    ones_in = dt("ones", [128, 128], F32, kind="ExternalInput").ap()
    out = dt("out", [TOWN, D], F32, kind="ExternalOutput").ap()

    sc_xnT_cov = dt("sc_xnT_cov", [8, 128, 16384], F32).ap()
    sc_xnT_own = dt("sc_xnT_own", [2, 128, 16384], F32).ap()
    sc_dq3T = dt("sc_dq3T", [NHT, 32, 128, 128], F32).ap()
    sc_hT = dt("sc_hT", [NHT, 128, 1024], F32).ap()
    sc_attn = dt("sc_attn", [NG, 8, 128, 4096], F32).ap()

    P = Plan()
    with contextlib.ExitStack() as es:
        sbuf = lambda name, cols: es.enter_context(nc.sbuf_tensor(name, [128, cols], F32))
        AR1 = sbuf("AR1", 24576)
        AR2 = sbuf("AR2", 16384)
        AR3 = sbuf("AR3", 5120)
        SM = sbuf("SM", 192)
        RPO = sbuf("RPO", 256)
        RPC = sbuf("RPC", 256)
        IDT = sbuf("IDT", 128)
        ONE = sbuf("ONE", 128)
        SUMS = sbuf("SUMS", 2048)
        pb = [es.enter_context(nc.psum_tensor("pb%d" % i, [128, 512], F32)) for i in range(8)]
        sem = {k: es.enter_context(nc.semaphore("sem_" + k))
               for k in ("ld", "st", "t", "v", "a", "sl")}
        block = es.enter_context(nc.Block())

        nc.const_aps.register(F32, EPS, SM[:, 8:9])

        def LD(dst, src, waits=(), qe="sync"):
            P.emit(qe, lambda e, d=dst, s=src: e.dma_start(out=d, in_=s), waits, True, dma=True)

        def ST(dst, src, waits=()):
            P.emit("gpsimd", lambda e, d=dst, s=src: e.dma_start(out=d, in_=s), waits, True)

        def LMARK(qe="sync"):
            sm = {"sync": "ld", "gpsimd": "st", "scalar": "sl"}[qe]
            P.emit(qe, lambda e: e.nop(), [(sm, P.c(sm))], False)
            return (sm, P.c(sm))

        def SMARK():
            P.emit("gpsimd", lambda e: e.nop(), [("st", P.c("st"))], False)
            return ("st", P.c("st"))

        TE = lambda fn, waits=(), inc=False: P.emit("tensor", fn, waits, inc)
        VE = lambda fn, waits=(), inc=False: P.emit("vector", fn, waits, inc)
        AE = lambda fn, waits=(), inc=False: P.emit("scalar", fn, waits, inc)

        VE(lambda e: e.memset(SM[:, 8:9], EPS))
        LD(IDT[:], ident_in)
        LD(ONE[:], ones_in)
        w_consts = [LMARK(), ("v", P.c("v"))]

        bx = AR1[:, 0:4096]
        bsq = AR1[:, 4096:8192]
        bxn = AR1[:, 8192:12288]
        basm = AR2[:, 0:16384]
        s_sum, s_sq = SM[:, 0:1], SM[:, 1:2]
        s_m, s_ex2 = SM[:, 2:3], SM[:, 3:4]
        s_mm, s_var = SM[:, 4:5], SM[:, 5:6]
        s_sd, s_r = SM[:, 6:7], SM[:, 7:8]
        s_mr, s_b = SM[:, 10:11], SM[:, 11:12]

        blocks = [("cov", kb) for kb in range(8)] + [("own", h) for h in range(2)]
        xnT_ready = {}
        for bid, (kind, idx) in enumerate(blocks):
            src = x_cov if kind == "cov" else x_own
            for ts_ in range(4):
                row0 = idx * 512 + ts_ * 128
                qe = "sync" if ts_ % 2 == 0 else "gpsimd"
                LD(bx, src[row0:row0 + 128, :],
                   waits=[("v", P.c("v")), ("a", P.c("a"))] + (w_consts if bid == 0 and ts_ == 0 else []),
                   qe=qe)
                wld = LMARK(qe)
                VE(lambda e: e.tensor_reduce(out=s_sum, in_=bx, axis=AX.X, op=OP.add),
                   waits=[wld])
                AE(lambda e: e.activation(bsq, bx, AF.Square),
                   waits=[wld, ("v", P.c("v"))], inc=True)
                VE(lambda e: e.tensor_reduce(out=s_sq, in_=bsq, axis=AX.X, op=OP.add),
                   waits=[("a", P.c("a"))])
                VE(lambda e: e.tensor_scalar_mul(s_m, s_sum, 1.0 / D))
                VE(lambda e: e.tensor_scalar_mul(s_ex2, s_sq, 1.0 / D))
                VE(lambda e: e.tensor_tensor(out=s_mm, in0=s_m, in1=s_m, op=OP.mult))
                VE(lambda e: e.tensor_tensor(out=s_var, in0=s_ex2, in1=s_mm, op=OP.subtract), inc=True)
                AE(lambda e: e.activation(s_sd, s_var, AF.Sqrt, bias=EPS),
                   waits=[("v", P.c("v"))], inc=True)
                VE(lambda e: e.reciprocal(s_r, s_sd), waits=[("a", P.c("a"))])
                VE(lambda e: e.tensor_tensor(out=s_mr, in0=s_m, in1=s_r, op=OP.mult))
                VE(lambda e: e.tensor_scalar_mul(s_b, s_mr, -1.0), inc=True)
                AE(lambda e: e.activation(bxn, bx, AF.Identity, bias=s_b, scale=s_r),
                   waits=[("v", P.c("v"))], inc=True)
                for grp in range(8):
                    for j in range(4):
                        ds = grp * 4 + j
                        TE(lambda e, j=j, ds=ds: e.transpose(
                            pb[6][:, j * 128:(j + 1) * 128], bxn[:, ds * 128:(ds + 1) * 128], IDT[:]),
                           waits=([("a", P.c("a"))] if j == 0 else []), inc=(j == 3))
                    wt = P.c("t")
                    for j in range(4):
                        ds = grp * 4 + j
                        cc = ds * 512 + ts_ * 128
                        AE(lambda e, j=j, cc=cc: e.copy(
                            basm[:, cc:cc + 128], pb[6][:, j * 128:(j + 1) * 128]),
                           waits=[("t", wt)], inc=(j == 3))
            dstb = (sc_xnT_cov if kind == "cov" else sc_xnT_own)[idx]
            ST(dstb, basm, waits=[("a", P.c("a"))])
            xnT_ready[(kind, idx)] = SMARK()
            if bid + 1 < len(blocks):
                AE(lambda e: e.nop(), waits=[xnT_ready[(kind, idx)]], inc=True)

        bwaits = [LMARK(), SMARK()] + [(k, P.c(k)) for k in ("t", "v", "a")]
        P.emit("sync", lambda e: e.nop(), [w for w in bwaits if w[0] != "lg"], False)
        P.emit("gpsimd", lambda e: e.nop(), [w for w in bwaits if w[0] != "sg"], False)
        P.emit("tensor", lambda e: e.nop(), [w for w in bwaits if w[0] != "t"], True)
        P.emit("vector", lambda e: e.nop(), [w for w in bwaits if w[0] != "v"], True)
        P.emit("scalar", lambda e: e.nop(), [w for w in bwaits if w[0] != "a"], True)

        bprobs = AR1[:, 8192:10240]
        m1b = bprobs[:, 0:512]
        m2b = bprobs[:, 512:1024]
        o1b = bprobs[:, 1024:1536]
        o2b = bprobs[:, 1536:2048]
        braw = AR1[:, 10240:12288]
        bvv = AR1[:, 12288:14336]
        btmp = AR1[:, 14336:16384]
        bacc = [AR1[:, 16384:18432], AR1[:, 18432:20480]]
        bdqa = AR1[:, 20480:24576]
        s_am = SM[:, 64:128]
        s_rc = SM[:, 128:192]
        qTb = AR1[:, 0:8192]
        kTb = AR2[:, 0:8192]
        vTb = AR2[:, 8192:16384]
        bwo = AR2[:, 0:8192]
        batn = AR2[:, 8192:12288]
        xa = [AR3[:, 0:512], AR3[:, 512:1024]]
        xb = [AR3[:, 1024:1536], AR3[:, 1536:2048]]
        wpc = [AR3[:, 2048:3072], AR3[:, 3072:4096]]
        bqro = AR3[:, 4096:4608]
        bmsk = AR3[:, 4608:5120]
        rop = [RPO[:, i * 64:(i + 1) * 64] for i in range(4)]
        rcv = [RPC[:, i * 64:(i + 1) * 64] for i in range(4)]
        attn_mark = [None]

        spill = {}
        dq_state = {"wt_hist": [0, 0], "chunk": 0}

        dqst = {"cons": [0, 0], "slot": 0, "stg": 0}

        def dequant_job(tag, idx, dst_asm=None, asm_guard=(), stg=None):
            if tag == "w1":
                src = w1
                chunks = [(0, 2048), (2048, 2048)]
            elif tag == "w2":
                src = w2
                chunks = [(0, 2048), (2048, 2048)]
            else:
                src = w3
                chunks = [(k * 2048, 2048) for k in range(5)] + [(10240, 768)]
            first_chunk = True
            for (c0, cw) in chunks:
                nb = cw // 64
                slot = dqst["slot"]
                dqst["slot"] = 1 - slot
                brw = braw[slot]
                qe = "scalar" if slot == 0 else "gpsimd"
                LD(brw[:, 0:cw], src[idx * 128:(idx + 1) * 128, c0:c0 + cw],
                   waits=[("v", dqst["cons"][slot])], qe=qe)
                wld = LMARK(qe)
                rawv = brw[:, 0:cw].rearrange("p (b j) -> p b j", j=64)
                VE(lambda e, rawv=rawv, nb=nb: e.tensor_reduce(
                    out=s_am[:, 0:nb], in_=rawv, axis=AX.X, op=OP.max, apply_absolute_value=True),
                   waits=[wld])
                VE(lambda e, nb=nb: e.reciprocal(s_rc[:, 0:nb], s_am[:, 0:nb]))
                rbc = s_rc[:, 0:nb].unsqueeze(2).broadcast_to((128, nb, 64))
                VE(lambda e, rawv=rawv, rbc=rbc, cw=cw: e.tensor_tensor(
                    out=bvv[:, 0:cw].rearrange("p (b j) -> p b j", j=64), in0=rawv, in1=rbc,
                    op=OP.mult), inc=True)
                dqst["cons"][slot] = P.c("v")
                pair = dq_state["chunk"] % 2
                dq_state["chunk"] += 1
                ba = [bacc[2 * pair], bacc[2 * pair + 1]]
                VE(lambda e, cw=cw: e.tensor_scalar(out=btmp[:, 0:cw], in0=bvv[:, 0:cw],
                                                    scalar1=float(BHI[0]), scalar2=float(BLO[0]),
                                                    op0=OP.subtract, op1=OP.is_gt))
                VE(lambda e, cw=cw, ba=ba: e.tensor_scalar_mul(ba[0][:, 0:cw], btmp[:, 0:cw],
                                                               float(DLT[0])),
                   waits=[("t", dq_state["wt_hist"][0])])
                a = 0
                for i in range(1, 15):
                    VE(lambda e, cw=cw, i=i: e.tensor_scalar(
                        out=btmp[:, 0:cw], in0=bvv[:, 0:cw], scalar1=float(BHI[i]),
                        scalar2=float(BLO[i]), op0=OP.subtract, op1=OP.is_gt))
                    VE(lambda e, cw=cw, i=i, a=a, ba=ba: e.scalar_tensor_tensor(
                        out=ba[1 - a][:, 0:cw], in0=btmp[:, 0:cw], scalar=float(DLT[i]),
                        in1=ba[a][:, 0:cw], op0=OP.mult, op1=OP.add))
                    a = 1 - a
                abc = s_am[:, 0:nb].unsqueeze(2).broadcast_to((128, nb, 64))
                VE(lambda e, cw=cw, a=a, abc=abc, ba=ba: e.scalar_tensor_tensor(
                    out=ba[1 - a][:, 0:cw].rearrange("p (b j) -> p b j", j=64),
                    in0=ba[a][:, 0:cw].rearrange("p (b j) -> p b j", j=64),
                    scalar=C0, in1=abc, op0=OP.add, op1=OP.mult), inc=True)
                wdq = P.c("v")
                dqbuf = ba[1 - a]
                npc = cw // 128
                for p0 in range(0, npc, 4):
                    n4 = min(4, npc - p0)
                    for j in range(n4):
                        TE(lambda e, j=j, dqbuf=dqbuf, p0=p0: e.transpose(
                            pb[7][:, j * 128:(j + 1) * 128],
                            dqbuf[:, (p0 + j) * 128:(p0 + j + 1) * 128], IDT[:]),
                           waits=([("v", wdq), ("a", P.c("a"))] if j == 0 else []),
                           inc=(j == n4 - 1))
                    wt = P.c("t")
                    ev_waits = [("t", wt)]
                    if tag != "w3":
                        if first_chunk and p0 == 0:
                            ev_waits.extend(asm_guard)
                        for j in range(n4):
                            gp = c0 // 128 + p0 + j
                            AE(lambda e, j=j, gp=gp, dst_asm=dst_asm: e.copy(
                                dst_asm[:, gp * 128:(gp + 1) * 128], pb[7][:, j * 128:(j + 1) * 128]),
                               waits=(ev_waits if j == 0 else []), inc=(j == n4 - 1))
                    else:
                        ev_waits.append(SMARK())
                        for j in range(n4):
                            AE(lambda e, j=j, stg=stg: e.copy(
                                stg[:, j * 128:(j + 1) * 128], pb[7][:, j * 128:(j + 1) * 128]),
                               waits=(ev_waits if j == 0 else []), inc=(j == n4 - 1))
                        wa = P.c("a")
                        for j in range(n4):
                            ht = c0 // 128 + p0 + j
                            ST(sc_dq3T[ht, idx], stg[:, j * 128:(j + 1) * 128], waits=[("a", wa)])
                dq_state["wt_hist"] = [dq_state["wt_hist"][1], P.c("t")]
                first_chunk = False
            if tag == "w3":
                spill[(tag, idx)] = SMARK()



        def rope_apply(mk, nh, tabs, dst, dstoff, waits0):
            ch, cl, sh, sl = tabs
            bc = lambda t: t.unsqueeze(1).broadcast_to((128, nh, 64))
            x1 = mk(0, 64)
            x2 = mk(64, 128)
            dview = dst[:, dstoff:dstoff + nh * 128].rearrange("p (h j) -> p h j", j=128)
            dst1 = dview[:, :, 0:64]
            dst2 = dview[:, :, 64:128]
            mm1 = m1b[:, 0:nh * 64].rearrange("p (h j) -> p h j", j=64)
            mm2 = m2b[:, 0:nh * 64].rearrange("p (h j) -> p h j", j=64)
            oo1 = o1b[:, 0:nh * 64].rearrange("p (h j) -> p h j", j=64)
            oo2 = o2b[:, 0:nh * 64].rearrange("p (h j) -> p h j", j=64)
            VE(lambda e: e.tensor_tensor(out=mm1, in0=x1, in1=bc(ch), op=OP.mult), waits=waits0)
            VE(lambda e: e.tensor_tensor(out=mm2, in0=x2, in1=bc(sh), op=OP.mult))
            VE(lambda e: e.tensor_tensor(out=oo1, in0=mm1, in1=mm2, op=OP.subtract))
            VE(lambda e: e.tensor_tensor(out=mm1, in0=x1, in1=bc(cl), op=OP.mult))
            VE(lambda e: e.tensor_tensor(out=mm2, in0=x2, in1=bc(sl), op=OP.mult))
            VE(lambda e: e.tensor_tensor(out=oo2, in0=mm1, in1=mm2, op=OP.subtract))
            VE(lambda e: e.tensor_tensor(out=dst1, in0=oo1, in1=oo2, op=OP.add))
            VE(lambda e: e.tensor_tensor(out=mm1, in0=x1, in1=bc(sh), op=OP.mult))
            VE(lambda e: e.tensor_tensor(out=mm2, in0=x2, in1=bc(ch), op=OP.mult))
            VE(lambda e: e.tensor_tensor(out=oo1, in0=mm1, in1=mm2, op=OP.add))
            VE(lambda e: e.tensor_tensor(out=mm1, in0=x1, in1=bc(sl), op=OP.mult))
            VE(lambda e: e.tensor_tensor(out=mm2, in0=x2, in1=bc(cl), op=OP.mult))
            VE(lambda e: e.tensor_tensor(out=oo2, in0=mm1, in1=mm2, op=OP.add))
            VE(lambda e: e.tensor_tensor(out=dst2, in0=oo1, in1=oo2, op=OP.add), inc=True)

        for g in range(NG):
            for pr in range(4):
                for ds in range(32):
                    rot = ds % 2
                    qe = "sync" if rot == 0 else "gpsimd"
                    LD(wpc[rot][:, 0:1024], Wq[g, ds], waits=[("t", P.c("t"))], qe=qe)
                    for ti in range(2):
                        tile = pr * 2 + ti
                        half, tsub = tile // 4, tile % 4
                        cc = ds * 512 + tsub * 128
                        LD((xa if ti == 0 else xb)[rot][:, 0:128],
                           sc_xnT_own[half][:, cc:cc + 128],
                           waits=[xnT_ready[("own", half)], ("t", P.c("t"))], qe=qe)
                    wld = LMARK(qe)
                    for ti in range(2):
                        xsrc = (xa if ti == 0 else xb)[rot][:, 0:128]
                        for nbk in range(2):
                            TE(lambda e, ti=ti, nbk=nbk, xsrc=xsrc, rot=rot, ds=ds: e.matmul(
                                pb[2 * ti + nbk][:], xsrc,
                                wpc[rot][:, nbk * 512:(nbk + 1) * 512],
                                start=(ds == 0), stop=(ds == 31)),
                               waits=([wld] +
                                      ([("v", P.c("v")), ("a", P.c("a"))] if ds == 0 else [])),
                               inc=(ti == 1 and nbk == 1))
                wmm = P.c("t")
                for ti in range(2):
                    tile = pr * 2 + ti
                    qe = "sync" if ti % 2 == 0 else "gpsimd"
                    for i in range(4):
                        LD(rop[i], rope_own[i, tile * 128:(tile + 1) * 128, :],
                           waits=[("v", P.c("v"))], qe=qe)
                    wld2 = LMARK(qe)
                    for bk in range(2):
                        bank = pb[2 * ti + bk]
                        mk = lambda lo, hi, bank=bank: bank[:, :].rearrange(
                            "p (h j) -> p h j", j=128)[:, :, lo:hi]
                        rope_apply(mk, 4, rop, bqro, 0,
                                   [wld2, ("t", wmm), ("a", P.c("a"))])
                        wv = P.c("v")
                        for j in range(4):
                            TE(lambda e, j=j: e.transpose(
                                pb[4][:, j * 128:(j + 1) * 128], bqro[:, j * 128:(j + 1) * 128],
                                IDT[:]),
                               waits=([("v", wv), ("a", P.c("a"))] if j == 0 else []),
                               inc=(j == 3))
                        wt = P.c("t")
                        for j in range(4):
                            h = bk * 4 + j
                            c1 = h * 1024 + tile * 128
                            AE(lambda e, j=j, c1=c1: e.copy(
                                qTb[:, c1:c1 + 128], pb[4][:, j * 128:(j + 1) * 128]),
                               waits=[("t", wt)], inc=(j == 3))
            for kb in range(8):
                for ds in range(32):
                    rot = ds % 2
                    qe = "sync" if rot == 0 else "gpsimd"
                    LD(wpc[rot][:, 0:512], Wkv[g, ds], waits=[("t", P.c("t"))], qe=qe)
                    LD(xa[rot][:, 0:512], sc_xnT_cov[kb][:, ds * 512:ds * 512 + 512],
                       waits=[xnT_ready[("cov", kb)], ("t", P.c("t"))], qe=qe)
                    wld = LMARK(qe)
                    for tks in range(4):
                        TE(lambda e, tks=tks, rot=rot, ds=ds: e.matmul(
                            pb[tks][:], xa[rot][:, tks * 128:(tks + 1) * 128],
                            wpc[rot][:, 0:512], start=(ds == 0), stop=(ds == 31)),
                           waits=([wld] +
                                  ([("v", P.c("v")), ("a", P.c("a"))] if ds == 0 else [])),
                           inc=(tks == 3))
                wmm = P.c("t")
                for tks in range(4):
                    qe = "sync" if tks % 2 == 0 else "gpsimd"
                    for i in range(4):
                        LD(rcv[i], rope_cov[i, kb * 512 + tks * 128: kb * 512 + (tks + 1) * 128, :],
                           waits=[("v", P.c("v"))], qe=qe)
                    wld2 = LMARK(qe)
                    bank = pb[tks]
                    mk = lambda lo, hi, bank=bank: bank[:, 0:256].rearrange(
                        "p (h j) -> p h j", j=128)[:, :, lo:hi]
                    rope_apply(mk, 2, rcv, bqro, 0, [wld2, ("t", wmm), ("a", P.c("a"))])
                    wv = P.c("v")
                    for j in range(2):
                        TE(lambda e, j=j: e.transpose(
                            pb[4][:, j * 128:(j + 1) * 128], bqro[:, j * 128:(j + 1) * 128], IDT[:]),
                           waits=([("v", wv), ("a", P.c("a"))] if j == 0 else []), inc=(j == 1))
                    wt = P.c("t")
                    for j in range(2):
                        c1 = j * 4096 + kb * 512 + tks * 128
                        AE(lambda e, j=j, c1=c1: e.copy(
                            kTb[:, c1:c1 + 128], pb[4][:, j * 128:(j + 1) * 128]),
                           waits=[("t", wt)] +
                                 ([attn_mark[0]] if (kb == 0 and tks == 0 and j == 0
                                                     and attn_mark[0]) else []),
                           inc=(j == 1))
                    for j in range(2):
                        c1 = ((kb * 4 + tks) * 2 + j) * 128
                        AE(lambda e, j=j, c1=c1, tks=tks: e.copy(
                            vTb[:, c1:c1 + 128], pb[tks][:, 256 + j * 128:256 + (j + 1) * 128]),
                           inc=(j == 1))
            bank_exp = [0, 0]
            last_mult_v = 0
            last_avsums_t = 0
            bpm = AR1[:, 10240:12288]
            mskblk = [AR1[:, 12288:14336], AR1[:, 14336:16384]]
            msk_v = [0, 0]
            blkctr = [0]
            for h in range(8):
                kvh = h // 4
                hfirst = [("a", P.c("a")), ("v", P.c("v"))]
                for qc in range(2):
                    nkb = 4 if qc == 0 else 8
                    first_blk = True
                    for kb in range(nkb):
                        blkid = kb if qc == 0 else 4 + kb
                        last_blk = (kb == nkb - 1)
                        mrot = blkctr[0] % 2
                        blkctr[0] += 1
                        qe = "sync" if mrot == 0 else "gpsimd"
                        for tks in range(4):
                            LD(mskblk[mrot][:, tks * 512:(tks + 1) * 512], m01[blkid, tks],
                               waits=([("v", msk_v[mrot])] if tks == 0 else []), qe=qe)
                        wmk = LMARK(qe)
                        exw = [0, 0]
                        for tks in range(4):
                            sbank = pb[tks % 2]
                            kcol = kvh * 4096 + kb * 512 + tks * 128
                            qcol = h * 1024 + qc * 512
                            TE(lambda e, sbank=sbank, kcol=kcol, qcol=qcol: e.matmul(
                                sbank[:], kTb[:, kcol:kcol + 128], qTb[:, qcol:qcol + 512],
                                start=True, stop=True),
                               waits=[("a", exw[tks % 2] or bank_exp[tks % 2])] +
                                     (hfirst if (first_blk and kb == 0 and tks == 0 and qc == 0) else []),
                               inc=True)
                            wsc = P.c("t")
                            AE(lambda e, sbank=sbank, tks=tks: e.activation(
                                bprobs[:, tks * 512:(tks + 1) * 512], sbank[:], AF.Exp),
                               waits=[("t", wsc), ("v", last_mult_v)], inc=True)
                            exw[tks % 2] = P.c("a")
                        bank_exp = exw[:]
                        VE(lambda e, mrot=mrot: e.tensor_tensor(
                            out=bpm, in0=bprobs[:, 0:2048], in1=mskblk[mrot], op=OP.mult),
                           waits=[("a", P.c("a")), wmk, ("t", last_avsums_t)], inc=True)
                        wpm = P.c("v")
                        last_mult_v = wpm
                        msk_v[mrot] = wpm
                        for tks in range(4):
                            first = first_blk and tks == 0
                            last = last_blk and tks == 3
                            vcol = ((kb * 4 + tks) * 2 + kvh) * 128
                            TE(lambda e, vcol=vcol, qc=qc, first=first, last=last, tks=tks: e.matmul(
                                pb[2 + qc][:], vTb[:, vcol:vcol + 128],
                                bpm[:, tks * 512:(tks + 1) * 512],
                                start=first, stop=last),
                               waits=([("v", wpm)] if tks == 0 else []), inc=True)
                            TE(lambda e, qc=qc, first=first, last=last, tks=tks: e.matmul(
                                pb[4 + qc][0:1, :], ONE[:, 0:1],
                                bpm[:, tks * 512:(tks + 1) * 512],
                                start=first, stop=last),
                               inc=True)
                        last_avsums_t = P.c("t")
                        first_blk = False
                wavd = P.c("t")
                AE(lambda e: e.copy(SUMS[0:1, 0:512], pb[4][0:1, :]), waits=[("t", wavd)])
                AE(lambda e: e.copy(SUMS[0:1, 512:1024], pb[5][0:1, :]), inc=True)
                VE(lambda e: e.reciprocal(SUMS[0:1, 1024:2048], SUMS[0:1, 0:1024]),
                   waits=[("a", P.c("a"))], inc=True)
                wrs = P.c("v")
                for qc in range(2):
                    TE(lambda e, qc=qc: e.matmul(
                        pb[qc][:], ONE[0:1, 0:128], SUMS[0:1, 1024 + qc * 512:1024 + (qc + 1) * 512],
                        start=True, stop=True),
                       waits=([("v", wrs), ("a", P.c("a"))] if qc == 0 else []), inc=(qc == 1))
                wbc = P.c("t")
                for qc in range(2):
                    AE(lambda e, qc=qc: e.copy(bprobs[:, qc * 512:(qc + 1) * 512], pb[qc][:]),
                       waits=([("t", wbc), ("v", P.c("v"))] if qc == 0 else []), inc=(qc == 1))
                wev = P.c("a")
                for qc in range(2):
                    c1 = h * 1024 + qc * 512
                    VE(lambda e, qc=qc, c1=c1: e.tensor_tensor(
                        out=qTb[:, c1:c1 + 512], in0=pb[2 + qc][:],
                        in1=bprobs[:, qc * 512:(qc + 1) * 512], op=OP.mult),
                       waits=[("a", wev)], inc=(qc == 1))
                last_mult_v = P.c("v")
            for tt_ in range(8):
                for rnd, (nb_, coff) in enumerate([(6, 0), (2, 3072)]):
                    for s in range(8):
                        rot = s % 2
                        qe = "sync" if rot == 0 else "gpsimd"
                        LD(bwo[:, rot * 4096: rot * 4096 + nb_ * 512],
                           Wo[g * 1024 + s * 128: g * 1024 + s * 128 + 128, coff:coff + nb_ * 512],
                           waits=[("t", P.c("t"))] +
                                 ([("v", P.c("v"))] if tt_ == 0 and rnd == 0 and s == 0 else []),
                           qe=qe)
                        wld = LMARK(qe)
                        for c in range(nb_):
                            lcol = (g * 0 + s) * 1024 + tt_ * 128
                            TE(lambda e, c=c, rot=rot, lcol=lcol, s=s: e.matmul(
                                pb[c][:], qTb[:, lcol:lcol + 128],
                                bwo[:, rot * 4096 + c * 512: rot * 4096 + (c + 1) * 512],
                                start=(s == 0), stop=(s == 7)),
                               waits=([wld] +
                                      ([("a", P.c("a")), ("v", P.c("v"))] if s == 0 else [])),
                               inc=(c == nb_ - 1))
                    wmm = P.c("t")
                    for c in range(nb_):
                        AE(lambda e, c=c, coff=coff: e.copy(
                            batn[:, coff + c * 512: coff + (c + 1) * 512], pb[c][:]),
                           waits=([("t", wmm)] + ([SMARK()] if c == 0 else [])),
                           inc=(c == nb_ - 1))
                ST(sc_attn[g, tt_], batn, waits=[("a", P.c("a"))])
            attn_mark[0] = SMARK()
        c_end_st = SMARK()

        braw = [AR1[:, 8192:10240], AR1[:, 10240:12288]]
        bvv = AR1[:, 12288:14336]
        btmp = AR1[:, 14336:16384]
        bacc = [AR1[:, 16384:18432], AR1[:, 18432:20480],
                AR1[:, 20480:22528], AR1[:, 22528:24576]]
        bdqa1r = [AR1[:, 0:4096], AR1[:, 4096:8192]]
        bdqa2r = [AR2[:, 8192:12288], AR2[:, 12288:16384]]
        bhD = AR3[:, 0:1024]
        bsilD = AR3[:, 1024:1536]
        w3stg = AR3[:, 1536:2048]
        xnq = [AR2[:, 0:4096], AR2[:, 4096:8192]]
        hT_ready = {}
        d_boundary = [LMARK(), c_end_st, ("a", P.c("a")), ("v", P.c("v")), ("t", P.c("t"))]
        asm_t = [0, 0]
        asm_ready = [0, 0]
        xnh_t = [0, 0]
        w3q = list(range(32))

        def emit_dq_ht(ht):
            slot = ht % 2
            guard = [("t", asm_t[slot])] + (d_boundary if ht == 0 else [])
            dequant_job("w1", ht, dst_asm=bdqa1r[slot], asm_guard=guard)
            dequant_job("w2", ht, dst_asm=bdqa2r[slot], asm_guard=[("t", asm_t[slot])])
            asm_ready[slot] = P.c("a")

        emit_dq_ht(0)
        for ht in range(NHT):
            slot = ht % 2
            wasm = asm_ready[slot]
            for half in range(2):
                for qtr in range(4):
                    rot = qtr % 2
                    qe = "sync"
                    LD(xnq[rot], sc_xnT_own[half][:, qtr * 4096:(qtr + 1) * 4096],
                       waits=[("t", xnh_t[rot])] +
                             (d_boundary if ht == 0 and half == 0 and qtr == 0 else []),
                       qe=qe)
                    wldq = LMARK(qe)
                    for dsl in range(8):
                        ds = qtr * 8 + dsl
                        TE(lambda e, ds=ds, dsl=dsl, rot=rot, half=half, slot=slot: e.matmul(
                            pb[half][:], bdqa1r[slot][:, ds * 128:(ds + 1) * 128],
                            xnq[rot][:, dsl * 512:(dsl + 1) * 512],
                            start=(ds == 0), stop=(ds == 31)),
                           waits=([wldq, ("a", wasm)] +
                                  ([("a", P.c("a")), ("v", P.c("v"))] if ds == 0 else [])))
                        TE(lambda e, ds=ds, dsl=dsl, rot=rot, half=half, slot=slot: e.matmul(
                            pb[2 + half][:], bdqa2r[slot][:, ds * 128:(ds + 1) * 128],
                            xnq[rot][:, dsl * 512:(dsl + 1) * 512],
                            start=(ds == 0), stop=(ds == 31)),
                           inc=True)
                    xnh_t[rot] = P.c("t")
            asm_t[slot] = P.c("t")
            for half in range(2):
                wmmD = P.c("t")
                AE(lambda e, half=half: e.activation(bsilD[:], pb[half][:], AF.Silu),
                   waits=[("t", wmmD), ("v", P.c("v"))], inc=True)
                VE(lambda e, half=half: e.tensor_tensor(
                    out=bhD[:, half * 512:(half + 1) * 512], in0=bsilD[:], in1=pb[2 + half][:],
                    op=OP.mult),
                   waits=[("a", P.c("a"))] +
                         ([hT_ready[ht - 1]] if (half == 0 and ht > 0) else []), inc=True)
            ST(sc_hT[ht], bhD, waits=[("v", P.c("v"))])
            hT_ready[ht] = SMARK()
            if ht + 1 < NHT:
                emit_dq_ht(ht + 1)
            if ht % 3 == 1 and w3q:
                dequant_job("w3", w3q.pop(0), stg=w3stg)
        while w3q:
            dequant_job("w3", w3q.pop(0), stg=w3stg)

        bhT = [AR1[:, 0:1024], AR1[:, 1024:2048]]
        bd3 = [AR1[:, 2048:2560], AR1[:, 2560:3072]]
        bep = [AR1[:, 3072 + k * 512: 3072 + (k + 1) * 512] for k in range(10)]
        bout = [AR1[:, 8192:8704], AR1[:, 8704:9216]]
        e_boundary = [LMARK(), ("a", P.c("a")), ("v", P.c("v"))]
        for d8 in range(8):
            for ht in range(NHT):
                rot = ht % 2
                qe = "sync" if rot == 0 else "gpsimd"
                LD(bhT[rot], sc_hT[ht], waits=[hT_ready[ht], ("t", P.c("t"))] +
                   (e_boundary if d8 == 0 and ht == 0 else []), qe=qe)
                for j in range(4):
                    LD(bd3[rot][:, j * 128:(j + 1) * 128], sc_dq3T[ht, d8 * 4 + j],
                       waits=([spill[("w3", d8 * 4 + 3)], ("t", P.c("t"))] if j == 0 else []), qe=qe)
                wld = LMARK(qe)
                for tt_ in range(8):
                    TE(lambda e, tt_=tt_, rot=rot, ht=ht: e.matmul(
                        pb[tt_][:], bhT[rot][:, tt_ * 128:(tt_ + 1) * 128], bd3[rot][:, 0:512],
                        start=(ht == 0), stop=(ht == NHT - 1)),
                       waits=([wld] +
                              ([("v", P.c("v")), ("a", P.c("a"))] if ht == 0 else [])),
                       inc=(tt_ == 7))
            wmm = P.c("t")
            for tt_ in range(8):
                rot = tt_ % 2
                qe = "sync" if rot == 0 else "gpsimd"
                LD(bep[5 * rot + 0], x_own[tt_ * 128:(tt_ + 1) * 128, d8 * 512:(d8 + 1) * 512],
                   waits=[("v", P.c("v"))], qe=qe)
                for g in range(NG):
                    LD(bep[5 * rot + 1 + g], sc_attn[g, tt_][:, d8 * 512:(d8 + 1) * 512],
                       waits=([c_end_st] if d8 == 0 and g == 0 else []), qe=qe)
                wld = LMARK(qe)
                VE(lambda e, tt_=tt_, rot=rot: e.tensor_tensor(
                    out=bout[rot], in0=pb[tt_][:], in1=bep[5 * rot + 0], op=OP.add),
                   waits=[wld, ("t", wmm), SMARK()])
                for g in range(NG):
                    VE(lambda e, g=g, rot=rot: e.tensor_tensor(
                        out=bout[rot], in0=bout[rot], in1=bep[5 * rot + 1 + g], op=OP.add),
                       inc=(g == NG - 1))
                wv = P.c("v")
                ST(out[tt_ * 128:(tt_ + 1) * 128, d8 * 512:(d8 + 1) * 512], bout[rot],
                   waits=[("v", wv)])
        P.emit("gpsimd", lambda e: e.nop(), [("st", P.c("st"))], False)

        def run(eng_name, proxy):
            own = P.esem[eng_name]
            self_serial = eng_name in ("vector", "scalar")
            idx = 0
            for waits, fn, isem, q in P.items[eng_name]:
                seen = {}
                if self_serial and idx > 0:
                    seen[own] = idx
                for s, val in waits:
                    if val > 0:
                        seen[s] = max(seen.get(s, 0), val)
                for s, val in seen.items():
                    proxy.wait_ge(sem[s], val)
                inst = fn(proxy)
                if isem is not None:
                    inst.then_inc(sem[isem], q)
                    if isem == own:
                        idx += q

        @block.sync
        def _(sync):
            run("sync", sync)

        @block.tensor
        def _(tensor):
            run("tensor", tensor)

        @block.vector
        def _(vector):
            run("vector", vector)

        @block.scalar
        def _(scalar):
            run("scalar", scalar)

        @block.gpsimd
        def _(gpsimd):
            run("gpsimd", gpsimd)

    return nc


def host_inputs(x, Wq, Wk, Wv, Wo, w1, w2, w3):
    inv = 500000.0 ** (-(np.arange(0, 128, 2, dtype=np.float64) / 128.0))
    pos = np.arange(T, dtype=np.float64)
    fr = np.outer(pos, inv)
    c64, s64 = np.cos(fr), np.sin(fr)
    scl = 1.0 / np.sqrt(np.float64(128.0))

    def hilo(a64):
        hi = a64.astype(np.float32)
        lo = (a64 - hi.astype(np.float64)).astype(np.float32)
        return hi, lo

    ch, cl = hilo(c64)
    sh, sl = hilo(s64)
    rope_cov = np.ascontiguousarray(np.stack([ch, cl, sh, sl]))
    chq, clq = hilo(c64 * scl)
    shq, slq = hilo(s64 * scl)
    rope_q = np.stack([chq, clq, shq, slq])

    Wq_t = np.zeros((NG, 32, 128, 1024), np.float32)
    Wkv_t = np.zeros((NG, 32, 128, 512), np.float32)
    for g in range(NG):
        for ds in range(32):
            Wq_t[g, ds] = Wq[ds * 128:(ds + 1) * 128, g * 1024:(g + 1) * 1024]
            Wkv_t[g, ds, :, 0:256] = Wk[ds * 128:(ds + 1) * 128, g * 256:(g + 1) * 256]
            Wkv_t[g, ds, :, 256:512] = Wv[ds * 128:(ds + 1) * 128, g * 256:(g + 1) * 256]

    ident = np.eye(128, dtype=np.float32)
    ones = np.ones((128, 128), dtype=np.float32)
    maps = []
    for core in range(8):
        b, p1 = core // 4, core % 4
        p2 = 7 - p1
        xb = np.asarray(x[b])
        x_own_ = np.concatenate([xb[p1 * 512:(p1 + 1) * 512], xb[p2 * 512:(p2 + 1) * 512]])
        ro = np.concatenate([rope_q[:, p1 * 512:(p1 + 1) * 512],
                             rope_q[:, p2 * 512:(p2 + 1) * 512]], axis=1)
        m01_ = np.zeros((12, 4, 128, 512), np.float32)
        for jj in range(12):
            qc, kb = (0, jj) if jj < 4 else (1, jj - 4)
            posq = (p1 if qc == 0 else p2) * 512 + np.arange(512)
            for tks in range(4):
                posk = kb * 512 + tks * 128 + np.arange(128)
                m01_[jj, tks] = (posk[:, None] <= posq[None, :]).astype(np.float32)
        maps.append(dict(
            x_own=np.ascontiguousarray(x_own_), x_cov=np.ascontiguousarray(xb),
            Wq=Wq_t, Wkv=Wkv_t, Wo=np.ascontiguousarray(Wo),
            w1=np.ascontiguousarray(w1), w2=np.ascontiguousarray(w2),
            w3=np.ascontiguousarray(w3),
            rope_own=np.ascontiguousarray(ro.astype(np.float32)), rope_cov=rope_cov,
            m01=m01_, ident=ident, ones=ones))
    return maps


def kernel(L0_attn_x, L0_attn_Wq, L0_attn_Wk, L0_attn_Wv, L0_attn_Wo,
           L0_ffn_w1, L0_ffn_w2, L0_ffn_w3):
    x = np.asarray(L0_attn_x, dtype=np.float32)
    maps = host_inputs(x, np.asarray(L0_attn_Wq), np.asarray(L0_attn_Wk),
                       np.asarray(L0_attn_Wv), np.asarray(L0_attn_Wo),
                       np.asarray(L0_ffn_w1), np.asarray(L0_ffn_w2), np.asarray(L0_ffn_w3))
    nc = build_program()
    res = run_bass_kernel_spmd(nc, maps, core_ids=list(range(8)))
    outp = np.empty((B, T, D), dtype=np.float32)
    for core in range(8):
        b, p1 = core // 4, core % 4
        p2 = 7 - p1
        o = res.results[core]["out"]
        outp[b, p1 * 512:(p1 + 1) * 512] = o[0:512]
        outp[b, p2 * 512:(p2 + 1) * 512] = o[512:1024]
    return outp
